# version 11
# baseline (speedup 1.0000x reference)
import numpy as np
import ml_dtypes
from contextlib import ExitStack

import concourse.bass as bass
import concourse.mybir as mybir
from concourse.bass_utils import run_bass_kernel_spmd

F32 = mybir.dt.float32
BF16 = mybir.dt.bfloat16
Alu = mybir.AluOpType
Act = mybir.ActivationFunctionType
AX = mybir.AxisListType

D = 1024
E = 2048
NST = 16
BETA = 0.6
EPS_RMS = 1e-5
EPS_LN = 1e-5

ENGS = ("sync", "tensor", "scalar", "vector", "gpsimd")


class Walk:
    def __init__(self):
        self.recs = {e: [] for e in ENGS}
        self.cnt = {e: 0 for e in ENGS}
        self.dmac = {e: 0 for e in ENGS}
        self.promised = {e: 0 for e in ENGS}
        self.selfwait = {e: 0 for e in ENGS}

    def _fix(self, waits):
        out = []
        for t in waits:
            if t is None:
                continue
            k, e2, v = t
            if k == "d":
                v = self.dmac[e2]
                self.promised[e2] = max(self.promised[e2], v)
                out.append(("d", e2, v))
            else:
                out.append(t)
        return out

    def op(self, eng, build, waits=()):
        ws = self._fix(waits)
        self.cnt[eng] += 1
        self.recs[eng].append((build, ws, False))
        return ("w", eng, self.cnt[eng])

    def dma(self, eng, build, waits=()):
        ws = self._fix(waits)
        if self.promised[eng] > self.selfwait[eng]:
            self.selfwait[eng] = self.promised[eng]
            ws.append(("d", eng, self.promised[eng]))
        self.dmac[eng] += 16
        self.recs[eng].append((build, ws, True))
        return ("d", eng, self.dmac[eng])

    def barrier_tokens(self):
        toks = [("w", e, self.cnt[e]) for e in ENGS if self.cnt[e] > 0]
        toks += [("d", e, self.dmac[e]) for e in ENGS if self.dmac[e] > 0]
        return toks


def build_program(L, T, dbg=False):
    assert L % T == 0 and T <= 256 and T % 128 == 0
    NCH = L // T
    NTI = T // 128
    TT = L // 128
    K_SEL = L // 2

    nc = bass.Bass("TRN2", target_bir_lowering=False, debug=False)

    def din(name, shape, dt=F32):
        return nc.dram_tensor(name, list(shape), dt, kind="ExternalInput").ap()

    def dout(name, shape, dt=F32):
        return nc.dram_tensor(name, list(shape), dt, kind="ExternalOutput").ap()

    def dint(name, shape, dt=F32):
        return nc.dram_tensor(name, list(shape), dt, kind="Internal").ap()

    xfm = din("xfm", [D, L])
    w_in = din("w_in", [2, 8, 32, 128, 128])
    w_out = din("w_out", [2, 16, 8, 128, 128])
    w_xp = din("w_xp", [2, 16, 128, 96])
    w_dt = din("w_dt", [2, 16, 64, 128])
    w_w1 = din("w_w1", [8, 16, 128, 128])
    w_w2 = din("w_w2", [16, 8, 128, 128])
    w_wg = din("w_wg", [8, 128, 8])
    w_b2 = din("w_b2", [8, 8, 128])
    cw_v = din("cw_v", [2, 16, 128, 4])
    cb_v = din("cb_v", [2, 16, 128, 1])
    dtb_v = din("dtb_v", [2, 16, 128, 1])
    dp_v = din("dp_v", [2, 16, 128, 1])
    nw_v = din("nw_v", [2, 8, 128, 1])
    alog_v = din("alog_v", [2, 16, 128, 16])
    b1_v = din("b1_v", [16, 128, 1])
    wgb_v = din("wgb_v", [8, 1])
    cbeta = din("cbeta", [128, 2 * (T + 1)])
    cones = din("cones", [128, T], BF16)
    cident = din("cident", [128, 128])
    cidentb = din("cidentb", [128, 128], BF16)
    cmisc = din("cmisc", [128, 2])

    dkind = dout if dbg else dint
    xresA = dkind("xresA", [D, L])
    xresB = dkind("xresB", [D, L])
    xmid = dkind("xmid", [D, L])
    bcrows = dint("bcrows", [32, L], BF16)
    selrow_d = dkind("selrow", [8, L], BF16)
    rowscr = dint("rowscr", [4, L])
    ofm = dout("ofm", [D, L])
    dbgs = {}
    if dbg:
        for nm, shape, dt_ in (("xz", [2 * E, T], BF16), ("xb", [E, T], BF16),
                               ("delta", [E, T], F32), ("y", [E, T], BF16),
                               ("np", [8, 4], F32), ("gt", [128, 8 * TT], F32)):
            dbgs[nm] = dout("dbg_" + nm, shape, dt_)

    es = ExitStack()

    def sbt(name, shape, dt=F32):
        return es.enter_context(nc.sbuf_tensor(name, list(shape), dt))

    def pst(name, shape, dt=F32):
        return es.enter_context(nc.psum_tensor(name, list(shape), dt))

    wa_in = sbt("wa_in", [128, 32768], BF16)
    wa_out = sbt("wa_out", [128, 16384], BF16)
    wa_xp = sbt("wa_xp", [128, 1536], BF16)
    wa_dt = sbt("wa_dt", [64, 2048], BF16)
    wa_g = sbt("wa_g", [128, 64])

    xf = sbt("xf", [128, 8 * T])
    xre = sbt("xre", [128, 2 * T])
    xl = sbt("xl", [128, 8 * T], BF16)
    xbp = sbt("xbp", [128, 16 * (T + 3)], BF16)
    xbt = sbt("xbt", [128, 16 * T], BF16)
    szt = sbt("szt", [128, 16 * T], BF16)
    dsb = sbt("dsb", [64, T], BF16)
    bcs = sbt("bcs", [32, T], BF16)
    dfl = sbt("dfl", [128, 2 * T])
    GW = 2 * (T + 1)
    dag = sbt("dag", [128, 2 * GW])
    mig = sbt("mig", [128, 2 * GW], BF16)
    vg = sbt("vg", [128, 2 * GW], BF16)
    hg = sbt("hg", [128, 2 * GW], BF16)
    hcl = sbt("hcl", [128, 2 * T], BF16)
    ubf = sbt("ubf", [128, 2 * T], BF16)
    yfl = sbt("yfl", [128, T])
    ybl = sbt("ybl", [128, 16 * T], BF16)
    bmc = sbt("bmc", [128, 32 * T], BF16)
    vcar = sbt("vcar", [128, 256], BF16)
    hcar = sbt("hcar", [128, 256], BF16)
    tmpa = sbt("tmpa", [128, T])
    tmpb = sbt("tmpb", [128, T])
    spa = sbt("spa", [128, T])
    spb = sbt("spb", [128, T])
    rowsb = sbt("rowsb", [1, 3 * T])
    invb = sbt("invb", [128, 2 * T])
    gt = sbt("gt", [128, 8 * TT])
    m2acc = sbt("m2acc", [128, 8])
    gtt = sbt("gtt", [128, 16])
    npsb = sbt("npsb", [8, 2])
    ecar = sbt("ecar", [8, 1])
    self32 = sbt("self32", [8, L], BF16)
    cvec = sbt("cvec", [128, 256])
    nega = sbt("nega", [128, 512])
    beta2 = sbt("beta2", [128, 2 * (T + 1)])
    onest = sbt("onest", [128, T], BF16)
    onecol = sbt("onecol", [128, 1])
    ident = sbt("ident", [128, 128])
    identb = sbt("identb", [128, 128], BF16)
    b1c = sbt("b1c", [128, 16])
    wgbc = sbt("wgbc", [8, 1])
    cmsb = sbt("cmsb", [128, 2])

    CW0, CB0, DTB0, DP0, NW0 = 0, 128, 160, 192, 224

    x0f = wa_out[:].bitcast(F32)[:, 1024:1024 + 8 * T]
    glg = wa_out[:].bitcast(F32)[0:8, 4608:4608 + T]
    nptt = wa_out[:].bitcast(F32)[0:8, 4864:4864 + 128]
    sbc = wa_out[:, 6144:6144 + 4 * T]
    wa_b2 = wa_out[0:8, 7168:7168 + 1024]
    ezb = wa_out[:].bitcast(F32)[0:8, 4096:4096 + 2 * T]

    pmm = pst("pmm", [128, 1024])
    pout = pst("pout", [128, 1024])
    pyy = pst("pyy", [128, 1024])
    paux = pst("paux", [128, 1024])

    S = {e: es.enter_context(nc.semaphore(f"S_{e}")) for e in ENGS}
    Q = {e: es.enter_context(nc.semaphore(f"Q_{e}")) for e in ENGS}

    W = Walk()
    wop = W.op
    wdma = W.dma
    DBGB = (lambda: ("d", "sync", 1)) if dbg else (lambda: None)

    t_cb = wdma("sync", lambda e: e.dma_start(out=beta2[:], in_=cbeta[:]))
    t_co = wdma("sync", lambda e: e.dma_start(out=onest[:], in_=cones[:]))
    t_id = wdma("sync", lambda e: e.dma_start(out=ident[:], in_=cident[:]))
    wdma("sync", lambda e: e.dma_start(out=identb[:], in_=cidentb[:]))
    wdma("sync", lambda e: e.dma_start(out=cmsb[:], in_=cmisc[:]))
    nc.const_aps.register(F32, EPS_RMS, cmsb[:, 0:1])
    nc.const_aps.register(F32, 1.0 / D, cmsb[:, 1:2])
    for l in range(2):
        for j in range(16):
            wdma("sync", lambda e, l=l, j=j: e.dma_start(
                out=cvec[:, CW0 + l * 64 + j * 4: CW0 + l * 64 + j * 4 + 4], in_=cw_v[l, j]))
            wdma("sync", lambda e, l=l, j=j: e.dma_start(
                out=cvec[:, CB0 + l * 16 + j: CB0 + l * 16 + j + 1], in_=cb_v[l, j]))
            wdma("sync", lambda e, l=l, j=j: e.dma_start(
                out=cvec[:, DTB0 + l * 16 + j: DTB0 + l * 16 + j + 1], in_=dtb_v[l, j]))
            wdma("sync", lambda e, l=l, j=j: e.dma_start(
                out=cvec[:, DP0 + l * 16 + j: DP0 + l * 16 + j + 1], in_=dp_v[l, j]))
            wdma("sync", lambda e, l=l, j=j: e.dma_start(
                out=nega[:, l * 256 + j * 16: l * 256 + j * 16 + 16], in_=alog_v[l, j]))
        for jd in range(8):
            wdma("sync", lambda e, l=l, jd=jd: e.dma_start(
                out=cvec[:, NW0 + l * 8 + jd: NW0 + l * 8 + jd + 1], in_=nw_v[l, jd]))
    for jh in range(16):
        wdma("sync", lambda e, jh=jh: e.dma_start(out=b1c[:, jh:jh + 1], in_=b1_v[jh]))
    t_wgb = wdma("sync", lambda e: e.dma_start(out=wgbc[:], in_=wgb_v[:]))
    t_vecs = ("d", "sync", W.dmac["sync"])

    for _e in ENGS:
        W.recs[_e].append((None, [t_vecs], None))
    t_onecol = wop("vector", lambda e: e.tensor_copy(onecol[:], onest[:, 0:1]), [t_co])
    t_na_e = wop("scalar", lambda e: e.activation(nega[:], nega[:], Act.Exp), [t_vecs])
    t_na = wop("vector", lambda e: e.tensor_scalar_mul(nega[:], nega[:], -1.0), [t_na_e])

    def load_mamba_weights(l, war_in=None, war_out=None, war_xpdt=None):
        first = [war_in]
        for k in range(8):
            for c in range(32):
                wdma("gpsimd", lambda e, l=l, k=k, c=c: e.dma_start(
                    out=wa_in[:, (k * 32 + c) * 128:(k * 32 + c + 1) * 128],
                    in_=w_in[l, k, c]), first)
                first = []
        first = [war_out]
        for k in range(16):
            for c in range(8):
                wdma("gpsimd", lambda e, l=l, k=k, c=c: e.dma_start(
                    out=wa_out[:, (k * 8 + c) * 128:(k * 8 + c + 1) * 128],
                    in_=w_out[l, k, c]), first)
                first = []
        first = [war_xpdt]
        for k in range(16):
            wdma("gpsimd", lambda e, l=l, k=k: e.dma_start(
                out=wa_xp[:, k * 96:(k + 1) * 96], in_=w_xp[l, k]), first)
            first = []
            wdma("gpsimd", lambda e, l=l, k=k: e.dma_start(
                out=wa_dt[:, k * 128:(k + 1) * 128], in_=w_dt[l, k]))
        return ("d", "gpsimd", W.dmac["gpsimd"])

    t_wl0 = load_mamba_weights(0)

    def mamba_layer(l, src_ap, dst_ap, w_tok, src_store_toks):
        entry = W.barrier_tokens()
        dst_store = [None] * NCH
        t_zc1 = wop("vector", lambda e: e.memset(vcar[:], 0.0), entry)
        t_zc2 = wop("vector", lambda e: e.memset(hcar[:], 0.0), entry)
        t_zdag = wop("vector", lambda e: e.memset(dag[:], 0.0), entry)
        t_zmig = wop("vector", lambda e: e.memset(mig[:], 0.0), entry)
        t_pad0 = wop("vector", lambda e: e.memset(
            xbp[:].rearrange("p (j t) -> p j t", j=16)[:, :, 0:3], 0.0), entry)

        prev_pad = [t_pad0] * 16
        t_x_prev_store = [None] * 8
        t_xl_prev_chunk = [None] * 8
        t_sq_prev_chunk = [None] * 8
        t_scan_tail = entry[0] if False else None
        t_evac31_prev = None
        BST = {"oev": None, "omm": None, "xre": None}
        pending_backend = None
        t_bmc_cons_v = None
        t_bmc_cons_g = None
        t_invb_cons_prev = None
        t_rs_prev = None
        t_sp_last_prev = None
        t_delta_prev = None
        prev_u = [None, None]
        prev_da15 = [None, None]
        t_umi_guard_c = [None, None]
        t_y2_slot_c = [None, None]
        t_ymm15_c = None
        prev_vsc_last = None
        prev_hsc_last = None
        prev_hcp_last = None
        prev_vext = [None] * 16
        t_hext_g = None
        t_dbgl = []
        ENT = dict(entry=entry)

        for c in range(NCH):
            co = c * T
            t_x = []
            for jd in range(8):
                t_x.append(wdma("sync", lambda e, jd=jd, co=co: e.dma_start(
                    out=xf[:, jd * T:(jd + 1) * T],
                    in_=src_ap[jd * 128:(jd + 1) * 128, co:co + T]),
                    [t_xl_prev_chunk[jd], t_sq_prev_chunk[jd],
                     src_store_toks[c] if src_store_toks is not None else None]
                    + (entry if (c == 0 and jd == 0) else [])))

            t_st = []
            t_sq_l = [None] * 8
            tq_prev = None
            for jd in range(8):
                buf = tmpa if jd % 2 == 0 else tmpb
                tq = wop("scalar", lambda e, jd=jd, buf=buf: e.activation(
                    buf[:], xf[:, jd * T:(jd + 1) * T], Act.Square),
                    [t_x[jd], t_st[jd - 2] if jd >= 2 else None,
                     t_sp_last_prev if jd < 2 else None])
                t_sq_l[jd] = tq
                t_st.append(wop("tensor", lambda e, jd=jd, buf=buf: e.matmul(
                    paux[0:1, 0:T], onecol[:], buf[:],
                    start=(jd == 0), stop=(jd == 7)),
                    [tq, t_onecol if jd == 0 else None]))
            t_inv_a = wop("scalar", lambda e: e.activation(
                rowsb[0:1, 0:T], paux[0:1, 0:T], Act.Sqrt, bias=EPS_RMS, scale=1.0 / D),
                [t_st[-1], t_rs_prev])
            t_inv = wop("vector", lambda e: e.reciprocal(
                rowsb[0:1, T:2 * T], rowsb[0:1, 0:T]), [t_inv_a, t_rs_prev])
            t_rs = wdma("sync", lambda e, co=co: e.dma_start(
                out=rowscr[0:1, co:co + T], in_=rowsb[0:1, T:2 * T]), [t_inv])
            t_rs_prev = t_rs
            t_bc = wdma("scalar", lambda e, co=co: e.dma_start(
                out=invb[:, 0:T],
                in_=rowscr[0:1, co:co + T].partition_broadcast(128)),
                [t_rs, t_invb_cons_prev])
            t_xl = []
            for jd in range(8):
                t_xl.append(wop("vector", lambda e, jd=jd: e.scalar_tensor_tensor(
                    xl[:, jd * T:(jd + 1) * T], xf[:, jd * T:(jd + 1) * T],
                    cvec[:, NW0 + l * 8 + jd: NW0 + l * 8 + jd + 1],
                    invb[:, 0:T], Alu.mult, Alu.mult),
                    [t_x[jd], t_bc, t_vecs if jd == 0 else None]))
            t_invb_cons_prev = t_xl[7]
            t_xl_prev_chunk = list(t_xl)
            t_sq_prev_chunk = list(t_sq_l)
            pending_backend = None

            t_evac = [None] * 32
            t_mm = None
            for c2 in range(32):
                po = (c2 % 2) * 512
                for k in range(8):
                    t_mm = wop("tensor", lambda e, c2=c2, k=k, po=po: e.matmul(
                        pmm[:, po:po + T],
                        wa_in[:, (k * 32 + c2) * 128:(k * 32 + c2 + 1) * 128],
                        xl[:, k * T:(k + 1) * T], start=(k == 0), stop=(k == 7)),
                        [w_tok if (c2 == 0 and k == 0) else None,
                         t_xl[k] if c2 == 0 else None,
                         (t_evac[c2 - 2] if c2 >= 2 else t_evac31_prev)
                         if k == 0 else None])
                if c2 < 16:
                    j = c2
                    if c > 0:
                        t_tail = wop("vector", lambda e, j=j: e.tensor_copy(
                            xbp[:, j * (T + 3):j * (T + 3) + 3],
                            xbp[:, j * (T + 3) + T:j * (T + 3) + T + 3]), [prev_pad[j]])
                    else:
                        t_tail = t_pad0
                    t_evac[c2] = wop("scalar", lambda e, j=j, po=po: e.activation(
                        xbp[:, j * (T + 3) + 3:j * (T + 3) + 3 + T],
                        pmm[:, po:po + T], Act.Copy), [t_mm, t_tail, DBGB()])
                else:
                    j = c2 - 16
                    t_evac[c2] = wop("scalar", lambda e, j=j, po=po: e.activation(
                        szt[:, j * T:(j + 1) * T], pmm[:, po:po + T], Act.Silu),
                        [t_mm, t_scan_tail if c2 == 16 else None, DBGB()])
            t_evac31_prev = t_evac[31]
            t_inproj_last = t_mm

            if dbg and l == 0 and c == 0:
                for c2 in range(16):
                    t_dbgl.append(wdma("sync", lambda e, c2=c2: e.dma_start(
                        out=dbgs["xz"][c2 * 128:(c2 + 1) * 128, 0:T],
                        in_=xbp[:, c2 * (T + 3) + 3:c2 * (T + 3) + 3 + T]), [t_evac[c2]]))
                for c2 in range(16):
                    t_dbgl.append(wdma("sync", lambda e, c2=c2: e.dma_start(
                        out=dbgs["xz"][(16 + c2) * 128:(17 + c2) * 128, 0:T],
                        in_=szt[:, c2 * T:(c2 + 1) * T]), [t_evac[16 + c2]]))

            t_xb = [None] * 16
            for j in range(16):
                base = j * (T + 3)
                buf = tmpa if j % 2 == 0 else tmpb
                tp = wop("vector", lambda e, base=base, j=j, buf=buf: e.tensor_scalar(
                    buf[:], xbp[:, base:base + T],
                    cvec[:, CW0 + l * 64 + j * 4: CW0 + l * 64 + j * 4 + 1], None, Alu.mult),
                    [t_evac[j], t_xb[j - 2] if j >= 2 else None,
                     t_scan_tail if j < 2 else None])
                for k in range(1, 4):
                    tp = wop("vector", lambda e, base=base, j=j, k=k, buf=buf:
                             e.scalar_tensor_tensor(
                                 buf[:], xbp[:, base + k:base + k + T],
                                 cvec[:, CW0 + l * 64 + j * 4 + k:
                                       CW0 + l * 64 + j * 4 + k + 1],
                                 buf[:], Alu.mult, Alu.add), [tp])
                t_xb[j] = wop("scalar", lambda e, j=j, buf=buf: e.activation(
                    xbt[:, j * T:(j + 1) * T], buf[:], Act.Silu,
                    bias=cvec[:, CB0 + l * 16 + j: CB0 + l * 16 + j + 1]),
                    [tp, t_scan_tail if j == 0 else None, DBGB()])
                prev_pad[j] = t_xb[j]
            if dbg and l == 0 and c == 0:
                for j in range(16):
                    t_dbgl.append(wdma("sync", lambda e, j=j: e.dma_start(
                        out=dbgs["xb"][j * 128:(j + 1) * 128, 0:T],
                        in_=xbt[:, j * T:(j + 1) * T]), [t_xb[j]]))

            t_xpm = None
            for k in range(16):
                t_xpm = wop("tensor", lambda e, k=k: e.matmul(
                    paux[0:96, 512:512 + T], wa_xp[:, k * 96:(k + 1) * 96],
                    xbt[:, k * T:(k + 1) * T], start=(k == 0), stop=(k == 15)),
                    [t_xb[k], w_tok if (c == 0 and k == 0) else None])
            t_dsb = wop("scalar", lambda e: e.activation(
                dsb[:], paux[0:64, 512:512 + T], Act.Copy), [t_xpm])
            t_bce = wop("scalar", lambda e: e.activation(
                bcs[:], paux[64:96, 512:512 + T], Act.Copy), [t_xpm])
            t_bcw = wdma("sync", lambda e, co=co: e.dma_start(
                out=bcrows[:, co:co + T], in_=bcs[:]), [t_bce])
            t_xp_last = t_xpm

            t_bmc = [None] * 32
            for n in range(32):
                t_bmc[n] = wdma("scalar", lambda e, n=n, co=co: e.dma_start(
                    out=bmc[:, n * T:(n + 1) * T],
                    in_=bcrows[n:n + 1, co:co + T].partition_broadcast(128)),
                    [t_bcw if n == 0 else None,
                     t_bmc_cons_v if n == 0 else None,
                     t_bmc_cons_g if n == 0 else None])

            t_y = [None] * 16
            t_vsc = [None] * 8
            t_hsc = [None] * 8
            t_hcp = [None] * 8
            t_delta = [None] * 16
            t_u = [None] * 16
            t_sp4 = None
            t_hc15 = None
            t_mi15 = None
            t_umi_guard = [t_umi_guard_c[0], t_umi_guard_c[1]]
            t_y2_slot = list(t_y2_slot_c)
            t_ymm15_prev = t_ymm15_c

            def stage_dt(j):
                nonlocal t_sp4
                dslot = (j % 2) * T
                t_dt = wop("tensor", lambda e, j=j: e.matmul(
                    paux[:, 0:T], wa_dt[:, j * 128:(j + 1) * 128], dsb[:],
                    start=True, stop=True),
                    [t_dsb if j == 0 else None,
                     w_tok if (c == 0 and j == 0) else None,
                     t_inv if j == 0 else None,
                     t_sp4 if t_sp4 is not None else None])
                dtb_col = cvec[:, DTB0 + l * 16 + j: DTB0 + l * 16 + j + 1]
                sp1 = wop("scalar", lambda e, dtb_col=dtb_col: e.activation(
                    spa[:], paux[:, 0:T], Act.Abs, bias=dtb_col),
                    [t_dt, t_delta[j - 1] if j > 0 else t_delta_prev])
                sp2 = wop("scalar", lambda e: e.activation(
                    spa[:], spa[:], Act.Exp, scale=-1.0), [sp1])
                sp3 = wop("scalar", lambda e: e.activation(
                    spa[:], spa[:], Act.Ln, bias=1.0), [sp2])
                t_sp4 = wop("scalar", lambda e, dtb_col=dtb_col: e.activation(
                    spb[:], paux[:, 0:T], Act.Relu, bias=dtb_col), [sp3])
                t_delta[j] = wop("vector", lambda e, dslot=dslot: e.tensor_tensor(
                    dfl[:, dslot:dslot + T], spa[:], spb[:], Alu.add),
                    [t_sp4, prev_da15[j % 2], prev_u[j % 2], DBGB()])
                if dbg and l == 0 and c == 0:
                    t_dbgl.append(wdma("sync", lambda e, j=j, dslot=dslot: e.dma_start(
                        out=dbgs["delta"][j * 128:(j + 1) * 128, 0:T],
                        in_=dfl[:, dslot:dslot + T]), [t_delta[j]]))
                t_u[j] = wop("vector", lambda e, j=j, dslot=dslot: e.tensor_tensor(
                    ubf[:, dslot:dslot + T], dfl[:, dslot:dslot + T],
                    xbt[:, j * T:(j + 1) * T], Alu.mult),
                    [t_delta[j], t_umi_guard[j % 2]])
                prev_u[j % 2] = t_u[j]

            stage_dt(0)
            for j in range(16):
                dslot = (j % 2) * T
                yslot = (j % 2) * 512
                t_hc = [None] * NST
                t_ymm = [None] * NST
                t_mi_last = None
                t_v_last = None
                t_h_last = None
                for g in range(8):
                    if g == 3 and j + 1 < 16:
                        stage_dt(j + 1)
                    n0 = 2 * g
                    n1 = 2 * g + 1
                    go = (g % 2) * GW
                    t_da0 = wop("scalar", lambda e, n0=n0, go=go, dslot=dslot, j=j:
                                e.activation(
                                    dag[:, go:go + T], dfl[:, dslot:dslot + T],
                                    Act.Exp,
                                    scale=nega[:, l * 256 + j * 16 + n0:
                                               l * 256 + j * 16 + n0 + 1]),
                                [t_delta[j],
                                 t_na if (c == 0 and j == 0 and g == 0) else None,
                                 t_zdag if (c == 0 and j == 0 and g < 2) else None,
                                 t_hsc[g - 2] if g >= 2 else
                                 (prev_hsc_last if g == 0 else None)])
                    t_da1 = wop("scalar", lambda e, n1=n1, go=go, dslot=dslot, j=j:
                                e.activation(
                                    dag[:, go + T + 1:go + 2 * T + 1],
                                    dfl[:, dslot:dslot + T], Act.Exp,
                                    scale=nega[:, l * 256 + j * 16 + n1:
                                               l * 256 + j * 16 + n1 + 1]),
                                [t_da0])
                    t_mi0 = wop("gpsimd", lambda e, n0=n0, go=go, dslot=dslot:
                                e.tensor_tensor(
                                    mig[:, go:go + T], ubf[:, dslot:dslot + T],
                                    bmc[:, n0 * T:(n0 + 1) * T], Alu.mult),
                                [t_u[j], t_bmc[n0],
                                 t_zmig if (c == 0 and j == 0 and g < 2) else None,
                                 t_vsc[g - 2] if g >= 2 else
                                 (prev_vsc_last if g == 0 else None)])
                    t_mi1 = wop("gpsimd", lambda e, n1=n1, go=go, dslot=dslot:
                                e.tensor_tensor(
                                    mig[:, go + T + 1:go + 2 * T + 1],
                                    ubf[:, dslot:dslot + T],
                                    bmc[:, n1 * T:(n1 + 1) * T], Alu.mult),
                                [t_mi0, t_bmc[n1]])
                    t_sv = wop("gpsimd", lambda e, n1=n1, go=go, j=j: e.tensor_copy(
                        mig[:, go + T:go + T + 1],
                        vcar[:, j * 16 + n1:j * 16 + n1 + 1]),
                        [t_mi1, prev_vext[j] if c > 0 else
                         (t_zc1 if j == 0 and g == 0 else None)])
                    t_vsc[g] = wop("vector", lambda e, go=go, n0=n0, j=j:
                                   e.tensor_tensor_scan(
                                       vg[:, go:go + GW], beta2[:],
                                       mig[:, go:go + GW],
                                       vcar[:, j * 16 + n0:j * 16 + n0 + 1],
                                       Alu.mult, Alu.add),
                                   [t_sv,
                                    t_cb if (c == 0 and j == 0 and g == 0) else None,
                                    t_hsc[g - 2] if g >= 2 else
                                    (prev_hsc_last if g == 0 else None)])
                    t_sh = wop("vector", lambda e, n1=n1, go=go, j=j: e.tensor_copy(
                        vg[:, go + T:go + T + 1],
                        hcar[:, j * 16 + n1:j * 16 + n1 + 1]),
                        [t_vsc[g], t_hext_g if c > 0 or j > 0 or g > 0 else t_zc2])
                    t_hsc[g] = wop("vector", lambda e, go=go, n0=n0, j=j:
                                   e.tensor_tensor_scan(
                                       hg[:, go:go + GW], dag[:, go:go + GW],
                                       vg[:, go:go + GW],
                                       hcar[:, j * 16 + n0:j * 16 + n0 + 1],
                                       Alu.mult, Alu.add),
                                   [t_sh, t_da1,
                                    t_hcp[g - 2] if g >= 2 else
                                    (prev_hcp_last if g == 0 else None)])
                    t_vext_g = wop("vector", lambda e, go=go, j=j, n0=n0: e.tensor_copy(
                        vcar[:, j * 16 + n0:j * 16 + n0 + 2],
                        vg[:, go + T - 1::T + 1][:, 0:2]), [t_hsc[g]])
                    t_hext_g = wop("vector", lambda e, go=go, j=j, n0=n0: e.tensor_copy(
                        hcar[:, j * 16 + n0:j * 16 + n0 + 2],
                        hg[:, go + T - 1::T + 1][:, 0:2]), [t_hsc[g]])
                    for s in range(2):
                        n = n0 + s
                        hc_o = (n % 2) * T
                        t_hc[n] = wop("vector", lambda e, n=n, go=go, s=s, hc_o=hc_o:
                                      e.tensor_tensor(
                                          hcl[:, hc_o:hc_o + T],
                                          hg[:, go + s * (T + 1):go + s * (T + 1) + T],
                                          bmc[:, (16 + n) * T:(17 + n) * T], Alu.mult),
                                      [t_hsc[g], t_bmc[16 + n],
                                       t_ymm[n - 2] if n >= 2 else
                                       (t_ymm15_prev if n == 0 else None)])
                        t_ymm[n] = wop("tensor", lambda e, n=n, hc_o=hc_o, yslot=yslot:
                                       e.matmul(
                                           pyy[:, yslot:yslot + T], identb[:],
                                           hcl[:, hc_o:hc_o + T],
                                           start=(n == 0), stop=(n == 15)),
                                       [t_hc[n],
                                        t_y2_slot[j % 2] if n == 0 else None])
                    t_hcp[g] = t_hc[n1]
                    t_mi_last = t_mi1
                prev_vsc_last = t_vsc[7]
                prev_hsc_last = t_hsc[7]
                prev_hcp_last = t_hc[15]
                prev_da15[j % 2] = t_da1
                prev_vext[j] = t_vext_g
                t_umi_guard[j % 2] = t_mi_last
                t_hc15 = t_hc[15]
                t_mi15 = t_mi_last
                t_ymm15_prev = t_ymm[15]

                t_y2 = wop("vector", lambda e, j=j, yslot=yslot: e.scalar_tensor_tensor(
                    yfl[:], xbt[:, j * T:(j + 1) * T],
                    cvec[:, DP0 + l * 16 + j: DP0 + l * 16 + j + 1],
                    pyy[:, yslot:yslot + T], Alu.mult, Alu.add), [t_ymm[15]])
                t_y2_slot[j % 2] = t_y2
                t_y[j] = wop("vector", lambda e, j=j: e.tensor_tensor(
                    ybl[:, j * T:(j + 1) * T], yfl[:], szt[:, j * T:(j + 1) * T],
                    Alu.mult),
                    [t_y2, t_evac[16 + j], BST["omm"] if j == 0 else None, DBGB()])
                if dbg and l == 0 and c == 0:
                    t_dbgl.append(wdma("sync", lambda e, j=j: e.dma_start(
                        out=dbgs["y"][j * 128:(j + 1) * 128, 0:T],
                        in_=ybl[:, j * T:(j + 1) * T]), [t_y[j]]))
            t_umi_guard_c = t_umi_guard
            t_y2_slot_c = t_y2_slot
            t_ymm15_c = t_ymm15_prev
            t_scan_tail = t_y[15]
            t_bmc_cons_v = t_hc15
            t_bmc_cons_g = t_mi15
            t_sp_last_prev = t_sp4

            def mk_backend(c=c, co=co, t_y15=t_y[15], ys_snap=tuple(t_y2_slot)):
                def backend():
                    t_om = None
                    t_ev = [None] * 8
                    for dt_i in range(8):
                        po = (dt_i % 2) * 512
                        ro = (dt_i % 2) * T
                        t_xr = wdma("sync", lambda e, dt_i=dt_i, co=co, ro=ro: e.dma_start(
                            out=xre[:, ro:ro + T],
                            in_=src_ap[dt_i * 128:(dt_i + 1) * 128, co:co + T]),
                            [t_ev[dt_i - 2] if dt_i >= 2 else BST["xre"]])
                        for j in range(16):
                            t_om = wop("tensor", lambda e, j=j, dt_i=dt_i, po=po: e.matmul(
                                pout[:, po:po + T],
                                wa_out[:, (j * 8 + dt_i) * 128:(j * 8 + dt_i + 1) * 128],
                                ybl[:, j * T:(j + 1) * T], start=(j == 0), stop=(j == 15)),
                                [t_y15 if (dt_i == 0 and j == 0) else None,
                                 (t_ev[dt_i - 2] if dt_i >= 2 else BST["oev"])
                                 if j == 0 else None])
                        t_ev[dt_i] = wop("vector", lambda e, dt_i=dt_i, po=po, ro=ro:
                                         e.scalar_tensor_tensor(
                                             xre[:, ro:ro + T], pout[:, po:po + T],
                                             1.0, xre[:, ro:ro + T],
                                             Alu.mult, Alu.add),
                                         [t_om, t_xr])
                        dst_store[c] = wdma("sync", lambda e, dt_i=dt_i, co=co, ro=ro:
                                            e.dma_start(
                            out=dst_ap[dt_i * 128:(dt_i + 1) * 128, co:co + T],
                            in_=xre[:, ro:ro + T]), [t_ev[dt_i]])
                    BST["xre"] = t_ev[7]
                    BST["oev"] = t_ev[7]
                    BST["omm"] = t_om
                return backend
            mk_backend()()
        return dst_store, t_sp_last_prev, t_inproj_last, BST["omm"], t_xp_last

    storeA, t_spA, t_li, t_lo, t_lx = mamba_layer(0, xfm, xresA, t_wl0, None)
    t_wl1 = load_mamba_weights(1, war_in=t_li, war_out=t_lo, war_xpdt=t_lx)
    storeB, t_spB, t_li, t_lo, t_lx = mamba_layer(1, xresA, xresB, t_wl1, storeA)

    t_war = ("w", "tensor", W.cnt["tensor"])
    first = [t_li]
    for k in range(8):
        for cc in range(16):
            wdma("gpsimd", lambda e, k=k, cc=cc: e.dma_start(
                out=wa_in[:, (k * 16 + cc) * 128:(k * 16 + cc + 1) * 128],
                in_=w_w1[k, cc]), first)
            first = []
    for k in range(16):
        for cc in range(8):
            wdma("gpsimd", lambda e, k=k, cc=cc: e.dma_start(
                out=wa_in[:, 16384 + (k * 8 + cc) * 128:16384 + (k * 8 + cc + 1) * 128],
                in_=w_w2[k, cc]))
    first = [t_lo]
    for k in range(8):
        wdma("gpsimd", lambda e, k=k: e.dma_start(out=wa_g[:, k * 8:(k + 1) * 8],
                                                  in_=w_wg[k]), first)
        first = []
    for cc in range(8):
        wdma("gpsimd", lambda e, cc=cc: e.dma_start(
            out=wa_b2[:, cc * 128:(cc + 1) * 128], in_=w_b2[cc]))
    t_wf = ("d", "gpsimd", W.dmac["gpsimd"])

    rowguard = {"r1": None, "r2": None}

    def postln(c, t_src, guards, invb_guard):
        co = c * T
        t_s1 = None
        t_s2 = None
        t_s2l = [None] * 8
        for jd in range(8):
            t_s1 = wop("tensor", lambda e, jd=jd: e.matmul(
                paux[0:1, 0:T], onecol[:], xf[:, jd * T:(jd + 1) * T],
                start=(jd == 0), stop=(jd == 7)),
                [t_src[jd]] + (guards if jd == 0 else []))
            buf = tmpa if jd % 2 == 0 else tmpb
            tq = wop("scalar", lambda e, jd=jd, buf=buf: e.activation(
                buf[:], xf[:, jd * T:(jd + 1) * T], Act.Square),
                [t_src[jd], t_s2l[jd - 2] if jd >= 2 else None])
            t_s2 = wop("tensor", lambda e, jd=jd, buf=buf: e.matmul(
                paux[0:1, 512:512 + T], onecol[:], buf[:],
                start=(jd == 0), stop=(jd == 7)),
                [tq] + (guards if jd == 0 else []))
            t_s2l[jd] = t_s2
        t_m = wop("vector", lambda e: e.tensor_scalar_mul(
            rowsb[0:1, 0:T], paux[0:1, 0:T], 1.0 / D),
            [t_s1, rowguard["r1"], rowguard["r2"]])
        t_v2 = wop("vector", lambda e: e.tensor_tensor(
            rowsb[0:1, T:2 * T], rowsb[0:1, 0:T], rowsb[0:1, 0:T], Alu.mult), [t_m])
        t_v3 = wop("vector", lambda e: e.scalar_tensor_tensor(
            rowsb[0:1, T:2 * T], paux[0:1, 512:512 + T], 1.0 / D,
            rowsb[0:1, T:2 * T], Alu.mult, Alu.subtract), [t_v2, t_s2])
        t_iv_a = wop("scalar", lambda e: e.activation(
            rowsb[0:1, 2 * T:3 * T], rowsb[0:1, T:2 * T], Act.Sqrt, bias=EPS_LN), [t_v3])
        t_iv = wop("vector", lambda e: e.reciprocal(
            rowsb[0:1, T:2 * T], rowsb[0:1, 2 * T:3 * T]), [t_iv_a])
        t_mi2 = wop("vector", lambda e: e.tensor_tensor(
            rowsb[0:1, 0:T], rowsb[0:1, 0:T], rowsb[0:1, T:2 * T], Alu.mult), [t_iv])
        t_r1 = wdma("sync", lambda e, co=co: e.dma_start(
            out=rowscr[1:2, co:co + T], in_=rowsb[0:1, T:2 * T]), [t_iv])
        t_r2 = wdma("sync", lambda e, co=co: e.dma_start(
            out=rowscr[2:3, co:co + T], in_=rowsb[0:1, 0:T]), [t_mi2])
        rowguard["r1"] = t_r1
        rowguard["r2"] = t_r2
        t_b1_ = wdma("scalar", lambda e, co=co: e.dma_start(
            out=invb[:, 0:T], in_=rowscr[1:2, co:co + T].partition_broadcast(128)),
            [t_r1, invb_guard])
        t_b2_ = wdma("scalar", lambda e, co=co: e.dma_start(
            out=invb[:, T:2 * T], in_=rowscr[2:3, co:co + T].partition_broadcast(128)),
            [t_r2])
        t_out = []
        for jd in range(8):
            t1 = wop("vector", lambda e, jd=jd: e.tensor_tensor(
                xf[:, jd * T:(jd + 1) * T], xf[:, jd * T:(jd + 1) * T],
                invb[:, 0:T], Alu.mult), [t_b1_, t_mi2 if jd == 0 else None])
            t2 = wop("vector", lambda e, jd=jd: e.tensor_tensor(
                xf[:, jd * T:(jd + 1) * T], xf[:, jd * T:(jd + 1) * T],
                invb[:, T:2 * T], Alu.subtract), [t1, t_b2_ if jd == 0 else None])
            t_out.append(t2)
        return t_out, t_v3, t_out[7]

    entry2 = W.barrier_tokens()
    t_xmid_store = [None] * NCH
    t_gtok = [None] * TT
    t_xf_store = [None] * 8
    t_x0_cons = None
    t_xl_cons = None
    t_xlc_prev = None
    t_gate_xf_cons = None
    t_invb_guard = ("w", "vector", W.cnt["vector"])
    sq_guards = list(entry2)
    for c in range(NCH):
        co = c * T
        t_x0 = []
        t_xs = []
        for jd in range(8):
            t_x0.append(wdma("sync", lambda e, jd=jd, co=co: e.dma_start(
                out=x0f[:, jd * T:(jd + 1) * T],
                in_=xfm[jd * 128:(jd + 1) * 128, co:co + T]),
                [t_x0_cons if jd == 0 else None]
                + (entry2 if (c == 0 and jd == 0) else [])))
            t_xs.append(wdma("sync", lambda e, jd=jd, co=co: e.dma_start(
                out=xf[:, jd * T:(jd + 1) * T],
                in_=xresB[jd * 128:(jd + 1) * 128, co:co + T]),
                [storeB[c] if jd == 0 else None, t_xf_store[jd],
                 t_xlc_prev if jd == 0 else None,
                 t_gate_xf_cons if jd == 0 else None]))
        t_s = []
        for jd in range(8):
            t_s.append(wop("vector", lambda e, jd=jd: e.tensor_tensor(
                xf[:, jd * T:(jd + 1) * T], xf[:, jd * T:(jd + 1) * T],
                x0f[:, jd * T:(jd + 1) * T], Alu.add), [t_x0[jd], t_xs[jd]]))
        t_x0_cons = t_s[7]
        t_xm, t_stats_free, t_ln_last = postln(c, t_s, sq_guards, t_invb_guard)
        t_invb_guard = t_ln_last
        sq_guards = []
        for jd in range(8):
            t_xf_store[jd] = wdma("sync", lambda e, jd=jd, co=co: e.dma_start(
                out=xmid[jd * 128:(jd + 1) * 128, co:co + T],
                in_=xf[:, jd * T:(jd + 1) * T]), [t_xm[jd]])
            wop("scalar", lambda e, jd=jd: e.activation(
                xl[:, jd * T:(jd + 1) * T], xf[:, jd * T:(jd + 1) * T], Act.Copy),
                [t_xm[jd], t_xl_cons if jd == 0 else None])
        t_xlc = ("w", "scalar", W.cnt["scalar"])
        t_xlc_prev = t_xlc
        t_xmid_store[c] = t_xf_store[7]

        t_g = None
        for k in range(8):
            t_g = wop("tensor", lambda e, k=k: e.matmul(
                paux[0:8, 512:512 + T], wa_g[:, k * 8:(k + 1) * 8],
                xf[:, k * T:(k + 1) * T], start=(k == 0), stop=(k == 7)),
                [t_xm[k],
                 t_wf if (c == 0 and k == 0) else None,
                 t_stats_free if k == 0 else None])
        t_xl_cons = t_g
        t_gate_xf_cons = t_g
        t_ge = wop("vector", lambda e: e.tensor_scalar_add(
            glg[:, 0:T], paux[0:8, 512:512 + T], wgbc[:]),
            [t_g, t_wgb if c == 0 else None])
        sq_guards = [t_ge]
        for r in range(NTI):
            rg = c * NTI + r
            t_tr = wop("tensor", lambda e, r=r: e.transpose(
                pmm[0:128, 0:8], glg[0:8, r * 128:(r + 1) * 128], ident[0:8, 0:8]),
                [t_ge, t_id if rg == 0 else None,
                 t_gtok[rg - 1] if rg >= 1 else None])
            t_gtok[rg] = wop("scalar", lambda e, rg=rg: e.activation(
                gt[:, rg * 8:(rg + 1) * 8], pmm[0:128, 0:8], Act.Copy), [t_tr])
    if dbg:
        wdma("sync", lambda e: e.dma_start(out=dbgs["gt"][:], in_=gt[:]),
             [t_gtok[TT - 1]])

    t_z = wop("vector", lambda e: e.memset(m2acc[:], 0.0))
    t_m2t = [None] * TT
    t_m2w = None
    t_ac = None
    for r in range(TT):
        t_mx = wop("vector", lambda e, r=r: e.tensor_reduce(
            gtt[:, 8:9], gt[:, r * 8:(r + 1) * 8], AX.X, Alu.max),
            [t_gtok[r], t_m2w, t_ac])
        t_m1 = wop("vector", lambda e, r=r: e.tensor_scalar(
            gtt[:, 0:8], gt[:, r * 8:(r + 1) * 8], gtt[:, 8:9], None, Alu.is_ge), [t_mx])
        t_gm = wop("vector", lambda e, r=r: e.scalar_tensor_tensor(
            gtt[:, 0:8], gtt[:, 0:8], -1e30, gt[:, r * 8:(r + 1) * 8],
            Alu.mult, Alu.add), [t_m1])
        t_mx2 = wop("vector", lambda e: e.tensor_reduce(
            gtt[:, 9:10], gtt[:, 0:8], AX.X, Alu.max), [t_gm])
        t_m2 = wop("vector", lambda e, r=r: e.tensor_scalar(
            gtt[:, 0:8], gt[:, r * 8:(r + 1) * 8], gtt[:, 9:10], None, Alu.is_ge),
            [t_mx2])
        t_ac = wop("vector", lambda e: e.tensor_tensor(
            m2acc[:], m2acc[:], gtt[:, 0:8], Alu.add), [t_m2, t_z if r == 0 else None])
        t_tr = wop("tensor", lambda e, r=r: e.transpose(
            pmm[0:8, 512:640], gtt[:, 0:8], ident[:]),
            [t_m2, t_m2t[r - 1] if r >= 1 else None])
        t_m2t[r] = wop("scalar", lambda e, r=r: e.activation(
            self32[:, r * 128:(r + 1) * 128], pmm[0:8, 512:640], Act.Copy), [t_tr])
        t_m2w = t_m2t[r]
    t_trn = wop("tensor", lambda e: e.transpose(pmm[0:8, 0:128], m2acc[:], ident[:]),
                [t_ac, t_m2t[TT - 1]])
    t_ne = wop("scalar", lambda e: e.activation(nptt[:], pmm[0:8, 0:128], Act.Copy),
               [t_trn])
    t_np = wop("vector", lambda e: e.tensor_reduce(
        npsb[:, 0:1], nptt[:], AX.X, Alu.add), [t_ne])
    t_def = wop("vector", lambda e: e.tensor_scalar(
        npsb[:, 1:2], npsb[:, 0:1], -1.0, float(K_SEL), Alu.mult, Alu.add), [t_np])
    if dbg:
        wdma("sync", lambda e: e.dma_start(out=dbgs["np"][:, 0:2], in_=npsb[:]), [t_def])
    t_selbf = [None] * NCH
    t_ec = None
    for c in range(NCH):
        co = c * T
        t_ez = wop("vector", lambda e, co=co: e.tensor_scalar(
            ezb[:, 0:T], self32[:, co:co + T], -1.0, 1.0, Alu.mult, Alu.add),
            [t_def if c == 0 else None])
        t_zc = wop("vector", lambda e, c=c: e.tensor_tensor_scan(
            ezb[:, T:2 * T], onest[0:8, 0:T], ezb[:, 0:T],
            (0.0 if c == 0 else ecar[:]), Alu.mult, Alu.add), [t_ez, t_ec])
        t_ec = wop("vector", lambda e: e.tensor_copy(ecar[:], ezb[:, 2 * T - 1:2 * T]),
                   [t_zc])
        t_zs = wop("vector", lambda e: e.tensor_scalar(
            ezb[:, T:2 * T], ezb[:, T:2 * T], npsb[:, 1:2], None, Alu.is_le), [t_ec])
        t_zs2 = wop("vector", lambda e: e.tensor_tensor(
            ezb[:, T:2 * T], ezb[:, T:2 * T], ezb[:, 0:T], Alu.mult), [t_zs])
        t_sel = wop("vector", lambda e, co=co: e.tensor_tensor(
            self32[:, co:co + T], self32[:, co:co + T], ezb[:, T:2 * T], Alu.add),
            [t_zs2])
        t_selbf[c] = wdma("sync", lambda e, co=co: e.dma_start(
            out=selrow_d[:, co:co + T], in_=self32[:, co:co + T]), [t_sel])

    entry3 = W.barrier_tokens()
    t_hev_prev = None
    t_w2_last_prev = None
    t_oev_guards = list(entry3)
    t_sbc_cons = None
    t_ost = [None] * 8
    t_invb_guard2 = t_invb_guard
    t_x0_g = t_x0_cons
    last_store = None
    for c in range(NCH):
        co = c * T
        t_xm_l = []
        t_x0_l = []
        for jd in range(8):
            t_xm_l.append(wdma("gpsimd", lambda e, jd=jd, co=co: e.dma_start(
                out=xl[:, jd * T:(jd + 1) * T],
                in_=xmid[jd * 128:(jd + 1) * 128, co:co + T]),
                [t_xmid_store[c] if jd == 0 else None,
                 t_xl_cons if jd == 0 else None]))
            t_x0_l.append(wdma("sync", lambda e, jd=jd, co=co: e.dma_start(
                out=x0f[:, jd * T:(jd + 1) * T],
                in_=xmid[jd * 128:(jd + 1) * 128, co:co + T]),
                [t_x0_g if jd == 0 else None]))
        t_sbc = [None] * 8
        for ee in range(4):
            t_sbc[ee] = wdma("scalar", lambda e, ee=ee, co=co: e.dma_start(
                out=sbc[:, (ee % 4) * T:(ee % 4 + 1) * T],
                in_=selrow_d[ee:ee + 1, co:co + T].partition_broadcast(128)),
                [t_selbf[c] if ee == 0 else None,
                 t_sbc_cons if ee == 0 else None])
        t_hm = [None] * 16
        for ht in range(16):
            po = (ht % 2) * 512
            t_mmh = None
            for k in range(8):
                t_mmh = wop("tensor", lambda e, ht=ht, k=k, po=po: e.matmul(
                    pmm[:, po:po + T],
                    wa_in[:, (k * 16 + ht) * 128:(k * 16 + ht + 1) * 128],
                    xl[:, k * T:(k + 1) * T], start=(k == 0), stop=(k == 7)),
                    [t_xm_l[k] if ht == 0 else None,
                     t_wf if (c == 0 and ht == 0 and k == 0) else None,
                     (t_hm[ht - 2] if ht >= 2 else t_hev_prev)
                     if k == 0 else None]
                    + (entry3 if (c == 0 and ht == 0 and k == 0) else []))
            t_he = wop("scalar", lambda e, ht=ht, po=po: e.activation(
                xbt[:, ht * T:(ht + 1) * T], pmm[:, po:po + T], Act.Gelu,
                bias=b1c[:, ht:ht + 1]),
                [t_mmh, t_vecs if (c == 0 and ht == 0) else None,
                 t_w2_last_prev if ht == 0 else None])
            t_hm[ht] = wop("vector", lambda e, ht=ht: e.tensor_tensor(
                xbt[:, ht * T:(ht + 1) * T], xbt[:, ht * T:(ht + 1) * T],
                sbc[:, ((ht // 2) % 4) * T:((ht // 2) % 4 + 1) * T], Alu.mult),
                [t_he, t_sbc[ht // 2]])
            if ht % 2 == 1 and ht // 2 + 4 < 8:
                ee2 = ht // 2 + 4
                t_sbc[ee2] = wdma("scalar", lambda e, ee2=ee2, co=co: e.dma_start(
                    out=sbc[:, (ee2 % 4) * T:(ee2 % 4 + 1) * T],
                    in_=selrow_d[ee2:ee2 + 1, co:co + T].partition_broadcast(128)),
                    [t_hm[ht]])
        t_hev_prev = t_hm[15]
        t_sbc_cons = t_hm[15]
        t_xl_cons = t_mmh
        t_w2 = None
        t_s2 = [None] * 8
        for dt_i in range(8):
            po = (dt_i % 2) * 512
            for k in range(16):
                t_w2 = wop("tensor", lambda e, dt_i=dt_i, k=k, po=po: e.matmul(
                    pout[:, po:po + T],
                    wa_in[:, 16384 + (k * 8 + dt_i) * 128:
                          16384 + (k * 8 + dt_i + 1) * 128],
                    xbt[:, k * T:(k + 1) * T], start=(k == 0), stop=False),
                    [t_hm[k] if dt_i == 0 else None,
                     (t_s2[dt_i - 2] if dt_i >= 2 else None) if k == 0 else None]
                    + (t_oev_guards if (dt_i == 0 and k == 0) else []))
            t_w2 = wop("tensor", lambda e, dt_i=dt_i, co=co, po=po: e.matmul(
                pout[:, po:po + T],
                wa_b2[:, dt_i * 128:(dt_i + 1) * 128],
                self32[:, co:co + T], start=False, stop=True),
                [t_selbf[c] if dt_i == 0 else None])
            t_s2[dt_i] = wop("vector", lambda e, dt_i=dt_i, po=po: e.scalar_tensor_tensor(
                xf[:, dt_i * T:(dt_i + 1) * T], pout[:, po:po + T],
                1.0, x0f[:, dt_i * T:(dt_i + 1) * T], Alu.mult, Alu.add),
                [t_w2, t_x0_l[dt_i], t_ost[dt_i] if c > 0 else None])
        t_w2_last_prev = t_w2
        t_x0_g = t_s2[7]
        t_fo, t_sf, t_ln_last = postln(c, t_s2, sq_guards, t_invb_guard2)
        sq_guards = []
        t_invb_guard2 = t_ln_last
        t_oev_guards = [t_ln_last]
        for jd in range(8):
            t_ost[jd] = wdma("sync", lambda e, jd=jd, co=co: e.dma_start(
                out=ofm[jd * 128:(jd + 1) * 128, co:co + T],
                in_=xf[:, jd * T:(jd + 1) * T]), [t_fo[jd]])
            last_store = t_ost[jd]

    for e in ENGS:
        if W.dmac[e] > 0:
            W.recs[e].append((None, [("d", e, W.dmac[e])], None))

    with es, nc.Block() as blk:
        def mkbody(eng_name):
            def body(eh):
                have = {}
                for key in [("w", x) for x in ENGS] + [("d", x) for x in ENGS]:
                    have[key] = 0
                for build, waits, isdma in W.recs[eng_name]:
                    for kind, peng, val in waits[:-1]:
                        if have[(kind, peng)] >= val:
                            continue
                        have[(kind, peng)] = val
                        sem = S[peng] if kind == "w" else Q[peng]
                        eh.wait_ge(sem, val)
                    attach = None
                    if waits:
                        kind, peng, val = waits[-1]
                        if have[(kind, peng)] < val:
                            have[(kind, peng)] = val
                            attach = (S[peng] if kind == "w" else Q[peng], val)
                    if build is None:
                        if attach is not None:
                            eh.wait_ge(attach[0], attach[1])
                        continue
                    inst = build(eh)
                    if attach is not None:
                        inst._wait_ge(attach[0], attach[1])
                    if isdma:
                        inst.then_inc(Q[eng_name], 16)
                    else:
                        inst.then_inc(S[eng_name], 1)
            return body

        blk.sync(mkbody("sync"))
        blk.tensor(mkbody("tensor"))
        blk.scalar(mkbody("scalar"))
        blk.vector(mkbody("vector"))
        blk.gpsimd(mkbody("gpsimd"))

    return nc


def _pack(inputs, L, T):
    f32 = np.float32
    ipw = np.asarray(inputs["L0_attn_in_proj_w"], f32)
    ow = np.asarray(inputs["L0_attn_out_w"], f32)
    xpw = np.asarray(inputs["L0_attn_x_proj_w"], f32)
    dtw = np.asarray(inputs["L0_attn_dt_w"], f32)
    W1 = np.asarray(inputs["L0_ffn_W1"], f32)
    W2 = np.asarray(inputs["L0_ffn_W2"], f32)
    wg = np.asarray(inputs["L0_ffn_w_gate_w"], f32)
    b2 = np.asarray(inputs["L0_ffn_b2"], f32)

    d = {}
    t = ipw.reshape(2, 32, 128, 8, 128)
    d["w_in"] = np.ascontiguousarray(t.transpose(0, 3, 1, 4, 2))
    t = ow.reshape(2, 8, 128, 16, 128)
    d["w_out"] = np.ascontiguousarray(t.transpose(0, 3, 1, 4, 2))
    t = xpw.reshape(2, 96, 16, 128)
    d["w_xp"] = np.ascontiguousarray(t.transpose(0, 2, 3, 1))
    t = dtw.reshape(2, 16, 128, 64)
    d["w_dt"] = np.ascontiguousarray(t.transpose(0, 1, 3, 2))
    W1c = np.ascontiguousarray(W1.transpose(1, 0, 2).reshape(1024, 2048))
    t = W1c.reshape(8, 128, 16, 128)
    d["w_w1"] = np.ascontiguousarray(t.transpose(0, 2, 1, 3))
    W2c = np.ascontiguousarray(W2.reshape(2048, 1024))
    t = W2c.reshape(16, 128, 8, 128)
    d["w_w2"] = np.ascontiguousarray(t.transpose(0, 2, 1, 3))
    d["w_wg"] = np.ascontiguousarray(wg.reshape(8, 128, 8))
    d["w_b2"] = np.ascontiguousarray(b2.reshape(8, 8, 128).transpose(1, 0, 2))

    d["cw_v"] = np.ascontiguousarray(
        np.asarray(inputs["L0_attn_conv_w"], f32).reshape(2, 16, 128, 4))
    d["cb_v"] = np.ascontiguousarray(
        np.asarray(inputs["L0_attn_conv_b"], f32).reshape(2, 16, 128, 1))
    d["dtb_v"] = np.ascontiguousarray(
        np.asarray(inputs["L0_attn_dt_b"], f32).reshape(2, 16, 128, 1))
    d["dp_v"] = np.ascontiguousarray(
        np.asarray(inputs["L0_attn_Dp"], f32).reshape(2, 16, 128, 1))
    d["nw_v"] = np.ascontiguousarray(
        np.asarray(inputs["L0_attn_norm_w"], f32).reshape(2, 8, 128, 1))
    d["alog_v"] = np.ascontiguousarray(
        np.asarray(inputs["L0_attn_A_log"], f32).reshape(2, 16, 128, 16))
    d["b1_v"] = np.ascontiguousarray(
        np.asarray(inputs["L0_ffn_b1"], f32).reshape(16, 128, 1))
    d["wgb_v"] = np.ascontiguousarray(
        np.asarray(inputs["L0_ffn_w_gate_b"], f32).reshape(8, 1))
    cb2 = np.full((128, 2 * (T + 1)), BETA, f32)
    cb2[:, T] = 0.0
    cb2[:, 2 * T + 1] = 0.0
    d["cbeta"] = cb2
    d["cmisc"] = np.ascontiguousarray(
        np.broadcast_to(np.array([1e-5, 1.0 / 1024.0], f32), (128, 2)))
    d["cones"] = np.ones((128, T), ml_dtypes.bfloat16)
    d["cident"] = np.eye(128, dtype=f32)
    d["cidentb"] = np.eye(128, dtype=ml_dtypes.bfloat16)
    return d


def kernel(**inputs):
    L = 4096
    T = 256
    x = np.asarray(inputs["L0_attn_x"], np.float32)
    B = x.shape[0]
    shared = _pack(inputs, L, T)
    nc = build_program(L, T, dbg=False)
    in_maps = []
    for b in range(B):
        m = dict(shared)
        m["xfm"] = np.ascontiguousarray(x[b].T)
        in_maps.append(m)
    res = run_bass_kernel_spmd(nc, in_maps, core_ids=list(range(B)))
    out = np.stack([np.asarray(r["ofm"]).T for r in res.results], axis=0)
    return np.ascontiguousarray(out.astype(np.float32))


# version 12
# speedup vs baseline: 1.2203x; 1.2203x over previous
import numpy as np
import ml_dtypes
from contextlib import ExitStack

import concourse.bass as bass
import concourse.mybir as mybir
from concourse.bass_utils import run_bass_kernel_spmd

F32 = mybir.dt.float32
BF16 = mybir.dt.bfloat16
Alu = mybir.AluOpType
Act = mybir.ActivationFunctionType
AX = mybir.AxisListType

D = 1024
E = 2048
NST = 16
BETA = 0.6
EPS_RMS = 1e-5
EPS_LN = 1e-5

ENGS = ("sync", "tensor", "scalar", "vector", "gpsimd")


class Walk:
    def __init__(self):
        self.recs = {e: [] for e in ENGS}
        self.cnt = {e: 0 for e in ENGS}
        self.dmac = {e: 0 for e in ENGS}
        self.promised = {e: 0 for e in ENGS}
        self.selfwait = {e: 0 for e in ENGS}

    def _fix(self, waits):
        out = []
        for t in waits:
            if t is None:
                continue
            k, e2, v = t
            if k == "d":
                v = self.dmac[e2]
                self.promised[e2] = max(self.promised[e2], v)
                out.append(("d", e2, v))
            else:
                out.append(t)
        return out

    def op(self, eng, build, waits=()):
        ws = self._fix(waits)
        self.cnt[eng] += 1
        self.recs[eng].append((build, ws, False))
        return ("w", eng, self.cnt[eng])

    def dma(self, eng, build, waits=()):
        ws = self._fix(waits)
        if self.promised[eng] > self.selfwait[eng]:
            self.selfwait[eng] = self.promised[eng]
            ws.append(("d", eng, self.promised[eng]))
        self.dmac[eng] += 16
        self.recs[eng].append((build, ws, True))
        return ("d", eng, self.dmac[eng])

    def barrier_tokens(self):
        toks = [("w", e, self.cnt[e]) for e in ENGS if self.cnt[e] > 0]
        toks += [("d", e, self.dmac[e]) for e in ENGS if self.dmac[e] > 0]
        return toks


def build_program(L, T, dbg=False):
    assert L % T == 0 and T <= 256 and T % 128 == 0
    NCH = L // T
    NTI = T // 128
    TT = L // 128
    K_SEL = L // 2

    nc = bass.Bass("TRN2", target_bir_lowering=False, debug=False)

    def din(name, shape, dt=F32):
        return nc.dram_tensor(name, list(shape), dt, kind="ExternalInput").ap()

    def dout(name, shape, dt=F32):
        return nc.dram_tensor(name, list(shape), dt, kind="ExternalOutput").ap()

    def dint(name, shape, dt=F32):
        return nc.dram_tensor(name, list(shape), dt, kind="Internal").ap()

    xfm = din("xfm", [D, L])
    w_in = din("w_in", [2, 8, 32, 128, 128])
    w_out = din("w_out", [2, 16, 8, 128, 128])
    w_xp = din("w_xp", [2, 16, 128, 96])
    w_dt = din("w_dt", [2, 16, 64, 128])
    w_w1 = din("w_w1", [8, 16, 128, 128])
    w_w2 = din("w_w2", [16, 8, 128, 128])
    w_wg = din("w_wg", [8, 128, 8])
    w_b2 = din("w_b2", [8, 8, 128])
    cw_v = din("cw_v", [2, 16, 128, 4])
    cb_v = din("cb_v", [2, 16, 128, 1])
    dtb_v = din("dtb_v", [2, 16, 128, 1])
    dp_v = din("dp_v", [2, 16, 128, 1])
    nw_v = din("nw_v", [2, 8, 128, 1])
    alog_v = din("alog_v", [2, 16, 128, 16])
    b1_v = din("b1_v", [16, 128, 1])
    wgb_v = din("wgb_v", [8, 1])
    cbeta = din("cbeta", [128, 2 * (T + 1)])
    cones = din("cones", [128, T], BF16)
    cident = din("cident", [128, 128])
    cidentb = din("cidentb", [128, 128], BF16)
    cmisc = din("cmisc", [128, 2])

    dkind = dout if dbg else dint
    xresA = dkind("xresA", [D, L])
    xresB = dkind("xresB", [D, L])
    xmid = dkind("xmid", [D, L])
    bcrows = dint("bcrows", [32, L], BF16)
    selrow_d = dkind("selrow", [8, L], BF16)
    rowscr = dint("rowscr", [4, L])
    ofm = dout("ofm", [D, L])
    dbgs = {}
    if dbg:
        for nm, shape, dt_ in (("xz", [2 * E, T], BF16), ("xb", [E, T], BF16),
                               ("delta", [E, T], F32), ("y", [E, T], BF16),
                               ("np", [8, 4], F32), ("gt", [128, 8 * TT], F32)):
            dbgs[nm] = dout("dbg_" + nm, shape, dt_)

    es = ExitStack()

    def sbt(name, shape, dt=F32):
        return es.enter_context(nc.sbuf_tensor(name, list(shape), dt))

    def pst(name, shape, dt=F32):
        return es.enter_context(nc.psum_tensor(name, list(shape), dt))

    wa_in = sbt("wa_in", [128, 32768], BF16)
    wa_out = sbt("wa_out", [128, 16384], BF16)
    wa_xp = sbt("wa_xp", [128, 1536], BF16)
    wa_dt = sbt("wa_dt", [64, 2048], BF16)
    wa_g = sbt("wa_g", [128, 64])

    xf = sbt("xf", [128, 8 * T])
    xre = sbt("xre", [128, 2 * T])
    xl = sbt("xl", [128, 8 * T], BF16)
    xbp = sbt("xbp", [128, 16 * (T + 3)], BF16)
    xbt = sbt("xbt", [128, 16 * T], BF16)
    szt = sbt("szt", [128, 16 * T], BF16)
    dsb = sbt("dsb", [64, T], BF16)
    bcs = sbt("bcs", [32, T], BF16)
    dfl = sbt("dfl", [128, 2 * T])
    GW = 2 * (T + 1)
    dag = sbt("dag", [128, 2 * GW])
    mig = sbt("mig", [128, 2 * GW], BF16)
    vg = sbt("vg", [128, 2 * GW], BF16)
    hg = sbt("hg", [128, 2 * GW], BF16)
    hcl = sbt("hcl", [128, 2 * T], BF16)
    ubf = sbt("ubf", [128, 2 * T], BF16)
    yfl = sbt("yfl", [128, T])
    ybl = sbt("ybl", [128, 16 * T], BF16)
    bmc = sbt("bmc", [128, 32 * T], BF16)
    vcar = sbt("vcar", [128, 256], BF16)
    hcar = sbt("hcar", [128, 256], BF16)
    tmpa = sbt("tmpa", [128, T])
    tmpb = sbt("tmpb", [128, T])
    spa = sbt("spa", [128, T])
    spb = sbt("spb", [128, T])
    rowsb = sbt("rowsb", [1, 3 * T])
    invb = sbt("invb", [128, 2 * T])
    gt = sbt("gt", [128, 8 * TT])
    m2acc = sbt("m2acc", [128, 8])
    gtt = sbt("gtt", [128, 16])
    npsb = sbt("npsb", [8, 2])
    ecar = sbt("ecar", [8, 1])
    self32 = sbt("self32", [8, L], BF16)
    cvec = sbt("cvec", [128, 256])
    nega = sbt("nega", [128, 512])
    beta2 = sbt("beta2", [128, 2 * (T + 1)])
    onest = sbt("onest", [128, T], BF16)
    onecol = sbt("onecol", [128, 1])
    ident = sbt("ident", [128, 128])
    identb = sbt("identb", [128, 128], BF16)
    b1c = sbt("b1c", [128, 16])
    wgbc = sbt("wgbc", [8, 1])
    cmsb = sbt("cmsb", [128, 2])

    CW0, CB0, DTB0, DP0, NW0 = 0, 128, 160, 192, 224

    x0f = wa_out[:].bitcast(F32)[:, 1024:1024 + 8 * T]
    glg = wa_out[:].bitcast(F32)[0:8, 4608:4608 + T]
    nptt = wa_out[:].bitcast(F32)[0:8, 4864:4864 + 128]
    sbc = wa_out[:, 6144:6144 + 4 * T]
    wa_b2 = wa_out[0:8, 7168:7168 + 1024]
    ezb = wa_out[:].bitcast(F32)[0:8, 4096:4096 + 2 * T]

    pmm = pst("pmm", [128, 1024])
    pout = pst("pout", [128, 1024])
    paux = pst("paux", [128, 1024])

    S = {e: es.enter_context(nc.semaphore(f"S_{e}")) for e in ENGS}
    Q = {e: es.enter_context(nc.semaphore(f"Q_{e}")) for e in ENGS}

    W = Walk()
    wop = W.op
    wdma = W.dma
    DBGB = (lambda: ("d", "sync", 1)) if dbg else (lambda: None)

    t_cb = wdma("sync", lambda e: e.dma_start(out=beta2[:], in_=cbeta[:]))
    t_co = wdma("sync", lambda e: e.dma_start(out=onest[:], in_=cones[:]))
    t_id = wdma("sync", lambda e: e.dma_start(out=ident[:], in_=cident[:]))
    wdma("sync", lambda e: e.dma_start(out=identb[:], in_=cidentb[:]))
    wdma("sync", lambda e: e.dma_start(out=cmsb[:], in_=cmisc[:]))
    nc.const_aps.register(F32, EPS_RMS, cmsb[:, 0:1])
    nc.const_aps.register(F32, 1.0 / D, cmsb[:, 1:2])
    for l in range(2):
        for j in range(16):
            wdma("sync", lambda e, l=l, j=j: e.dma_start(
                out=cvec[:, CW0 + l * 64 + j * 4: CW0 + l * 64 + j * 4 + 4], in_=cw_v[l, j]))
            wdma("sync", lambda e, l=l, j=j: e.dma_start(
                out=cvec[:, CB0 + l * 16 + j: CB0 + l * 16 + j + 1], in_=cb_v[l, j]))
            wdma("sync", lambda e, l=l, j=j: e.dma_start(
                out=cvec[:, DTB0 + l * 16 + j: DTB0 + l * 16 + j + 1], in_=dtb_v[l, j]))
            wdma("sync", lambda e, l=l, j=j: e.dma_start(
                out=cvec[:, DP0 + l * 16 + j: DP0 + l * 16 + j + 1], in_=dp_v[l, j]))
            wdma("sync", lambda e, l=l, j=j: e.dma_start(
                out=nega[:, l * 256 + j * 16: l * 256 + j * 16 + 16], in_=alog_v[l, j]))
        for jd in range(8):
            wdma("sync", lambda e, l=l, jd=jd: e.dma_start(
                out=cvec[:, NW0 + l * 8 + jd: NW0 + l * 8 + jd + 1], in_=nw_v[l, jd]))
    for jh in range(16):
        wdma("sync", lambda e, jh=jh: e.dma_start(out=b1c[:, jh:jh + 1], in_=b1_v[jh]))
    t_wgb = wdma("sync", lambda e: e.dma_start(out=wgbc[:], in_=wgb_v[:]))
    t_vecs = ("d", "sync", W.dmac["sync"])

    for _e in ENGS:
        W.recs[_e].append((None, [t_vecs], None))
    t_onecol = wop("vector", lambda e: e.tensor_copy(onecol[:], onest[:, 0:1]), [t_co])
    t_na_e = wop("scalar", lambda e: e.activation(nega[:], nega[:], Act.Exp), [t_vecs])
    t_na = wop("vector", lambda e: e.tensor_scalar_mul(nega[:], nega[:], -1.0), [t_na_e])

    def load_mamba_weights(l, war_in=None, war_out=None, war_xpdt=None):
        first = [war_in]
        for k in range(8):
            for c in range(32):
                wdma("gpsimd", lambda e, l=l, k=k, c=c: e.dma_start(
                    out=wa_in[:, (k * 32 + c) * 128:(k * 32 + c + 1) * 128],
                    in_=w_in[l, k, c]), first)
                first = []
        first = [war_out]
        for k in range(16):
            for c in range(8):
                wdma("gpsimd", lambda e, l=l, k=k, c=c: e.dma_start(
                    out=wa_out[:, (k * 8 + c) * 128:(k * 8 + c + 1) * 128],
                    in_=w_out[l, k, c]), first)
                first = []
        first = [war_xpdt]
        for k in range(16):
            wdma("gpsimd", lambda e, l=l, k=k: e.dma_start(
                out=wa_xp[:, k * 96:(k + 1) * 96], in_=w_xp[l, k]), first)
            first = []
            wdma("gpsimd", lambda e, l=l, k=k: e.dma_start(
                out=wa_dt[:, k * 128:(k + 1) * 128], in_=w_dt[l, k]))
        return ("d", "gpsimd", W.dmac["gpsimd"])

    t_wl0 = load_mamba_weights(0)

    def mamba_layer(l, src_ap, dst_ap, w_tok, src_store_toks):
        entry = W.barrier_tokens()
        dst_store = [None] * NCH
        t_zc1 = wop("vector", lambda e: e.memset(vcar[:], 0.0), entry)
        t_zc2 = wop("vector", lambda e: e.memset(hcar[:], 0.0), entry)
        t_zdag = wop("vector", lambda e: e.memset(dag[:], 0.0), entry)
        t_zmig = wop("vector", lambda e: e.memset(mig[:], 0.0), entry)
        t_pad0 = wop("vector", lambda e: e.memset(
            xbp[:].rearrange("p (j t) -> p j t", j=16)[:, :, 0:3], 0.0), entry)

        prev_pad = [t_pad0] * 16
        t_x_prev_store = [None] * 8
        t_xl_prev_chunk = [None] * 8
        t_sq_prev_chunk = [None] * 8
        t_scan_tail = entry[0] if False else None
        t_evac31_prev = None
        t_outevac_prev = None
        t_outmm_prev = None
        t_xre_guard = None
        t_bmc_cons_v = None
        t_bmc_cons_g = None
        t_invb_cons_prev = None
        t_rs_prev = None
        t_sp_last_prev = None
        t_delta_prev = None
        prev_u = [None, None]
        prev_da15 = [None, None]
        t_umi_guard_c = [None, None]
        t_y2_slot_c = [None, None]
        t_ymm15_c = None
        prev_vsc_last = None
        prev_hsc_last = None
        prev_hcp_last = None
        prev_vext = [None] * 16
        t_hext_g = None
        t_dbgl = []
        ENT = dict(entry=entry)

        for c in range(NCH):
            co = c * T
            t_x = []
            for jd in range(8):
                t_x.append(wdma("sync", lambda e, jd=jd, co=co: e.dma_start(
                    out=xf[:, jd * T:(jd + 1) * T],
                    in_=src_ap[jd * 128:(jd + 1) * 128, co:co + T]),
                    [t_xl_prev_chunk[jd], t_sq_prev_chunk[jd],
                     src_store_toks[c] if src_store_toks is not None else None]
                    + (entry if (c == 0 and jd == 0) else [])))

            t_st = []
            t_sq_l = [None] * 8
            tq_prev = None
            for jd in range(8):
                buf = tmpa if jd % 2 == 0 else tmpb
                tq = wop("scalar", lambda e, jd=jd, buf=buf: e.activation(
                    buf[:], xf[:, jd * T:(jd + 1) * T], Act.Square),
                    [t_x[jd], t_st[jd - 2] if jd >= 2 else None,
                     t_sp_last_prev if jd < 2 else None])
                t_sq_l[jd] = tq
                t_st.append(wop("tensor", lambda e, jd=jd, buf=buf: e.matmul(
                    paux[0:1, 0:T], onecol[:], buf[:],
                    start=(jd == 0), stop=(jd == 7)),
                    [tq, t_onecol if jd == 0 else None]))
            t_inv_a = wop("scalar", lambda e: e.activation(
                rowsb[0:1, 0:T], paux[0:1, 0:T], Act.Sqrt, bias=EPS_RMS, scale=1.0 / D),
                [t_st[-1], t_rs_prev])
            t_inv = wop("vector", lambda e: e.reciprocal(
                rowsb[0:1, T:2 * T], rowsb[0:1, 0:T]), [t_inv_a, t_rs_prev])
            t_rs = wdma("sync", lambda e, co=co: e.dma_start(
                out=rowscr[0:1, co:co + T], in_=rowsb[0:1, T:2 * T]), [t_inv])
            t_rs_prev = t_rs
            t_bc = wdma("scalar", lambda e, co=co: e.dma_start(
                out=invb[:, 0:T],
                in_=rowscr[0:1, co:co + T].partition_broadcast(128)),
                [t_rs, t_invb_cons_prev])
            t_xl = []
            for jd in range(8):
                t_xl.append(wop("vector", lambda e, jd=jd: e.scalar_tensor_tensor(
                    xl[:, jd * T:(jd + 1) * T], xf[:, jd * T:(jd + 1) * T],
                    cvec[:, NW0 + l * 8 + jd: NW0 + l * 8 + jd + 1],
                    invb[:, 0:T], Alu.mult, Alu.mult),
                    [t_x[jd], t_bc, t_vecs if jd == 0 else None]))
            t_invb_cons_prev = t_xl[7]
            t_xl_prev_chunk = list(t_xl)
            t_sq_prev_chunk = list(t_sq_l)

            t_evac = [None] * 32
            t_mm = None
            for c2 in range(32):
                po = (c2 % 2) * 512
                for k in range(8):
                    t_mm = wop("tensor", lambda e, c2=c2, k=k, po=po: e.matmul(
                        pmm[:, po:po + T],
                        wa_in[:, (k * 32 + c2) * 128:(k * 32 + c2 + 1) * 128],
                        xl[:, k * T:(k + 1) * T], start=(k == 0), stop=(k == 7)),
                        [w_tok if (c2 == 0 and k == 0) else None,
                         t_xl[k] if c2 == 0 else None,
                         (t_evac[c2 - 2] if c2 >= 2 else t_evac31_prev)
                         if k == 0 else None])
                if c2 < 16:
                    j = c2
                    if c > 0:
                        t_tail = wop("vector", lambda e, j=j: e.tensor_copy(
                            xbp[:, j * (T + 3):j * (T + 3) + 3],
                            xbp[:, j * (T + 3) + T:j * (T + 3) + T + 3]), [prev_pad[j]])
                    else:
                        t_tail = t_pad0
                    t_evac[c2] = wop("scalar", lambda e, j=j, po=po: e.activation(
                        xbp[:, j * (T + 3) + 3:j * (T + 3) + 3 + T],
                        pmm[:, po:po + T], Act.Copy), [t_mm, t_tail, DBGB()])
                else:
                    j = c2 - 16
                    t_evac[c2] = wop("scalar", lambda e, j=j, po=po: e.activation(
                        szt[:, j * T:(j + 1) * T], pmm[:, po:po + T], Act.Silu),
                        [t_mm, t_scan_tail if c2 == 16 else None, DBGB()])
            t_evac31_prev = t_evac[31]
            t_inproj_last = t_mm

            if dbg and l == 0 and c == 0:
                for c2 in range(16):
                    t_dbgl.append(wdma("sync", lambda e, c2=c2: e.dma_start(
                        out=dbgs["xz"][c2 * 128:(c2 + 1) * 128, 0:T],
                        in_=xbp[:, c2 * (T + 3) + 3:c2 * (T + 3) + 3 + T]), [t_evac[c2]]))
                for c2 in range(16):
                    t_dbgl.append(wdma("sync", lambda e, c2=c2: e.dma_start(
                        out=dbgs["xz"][(16 + c2) * 128:(17 + c2) * 128, 0:T],
                        in_=szt[:, c2 * T:(c2 + 1) * T]), [t_evac[16 + c2]]))

            t_xb = [None] * 16
            for j in range(16):
                base = j * (T + 3)
                buf = tmpa if j % 2 == 0 else tmpb
                tp = wop("vector", lambda e, base=base, j=j, buf=buf: e.tensor_scalar(
                    buf[:], xbp[:, base:base + T],
                    cvec[:, CW0 + l * 64 + j * 4: CW0 + l * 64 + j * 4 + 1], None, Alu.mult),
                    [t_evac[j], t_xb[j - 2] if j >= 2 else None,
                     t_scan_tail if j < 2 else None])
                for k in range(1, 4):
                    tp = wop("vector", lambda e, base=base, j=j, k=k, buf=buf:
                             e.scalar_tensor_tensor(
                                 buf[:], xbp[:, base + k:base + k + T],
                                 cvec[:, CW0 + l * 64 + j * 4 + k:
                                       CW0 + l * 64 + j * 4 + k + 1],
                                 buf[:], Alu.mult, Alu.add), [tp])
                t_xb[j] = wop("scalar", lambda e, j=j, buf=buf: e.activation(
                    xbt[:, j * T:(j + 1) * T], buf[:], Act.Silu,
                    bias=cvec[:, CB0 + l * 16 + j: CB0 + l * 16 + j + 1]),
                    [tp, t_scan_tail if j == 0 else None, DBGB()])
                prev_pad[j] = t_xb[j]
            if dbg and l == 0 and c == 0:
                for j in range(16):
                    t_dbgl.append(wdma("sync", lambda e, j=j: e.dma_start(
                        out=dbgs["xb"][j * 128:(j + 1) * 128, 0:T],
                        in_=xbt[:, j * T:(j + 1) * T]), [t_xb[j]]))

            t_xpm = None
            for k in range(16):
                t_xpm = wop("tensor", lambda e, k=k: e.matmul(
                    paux[0:96, 512:512 + T], wa_xp[:, k * 96:(k + 1) * 96],
                    xbt[:, k * T:(k + 1) * T], start=(k == 0), stop=(k == 15)),
                    [t_xb[k], w_tok if (c == 0 and k == 0) else None])
            t_dsb = wop("scalar", lambda e: e.activation(
                dsb[:], paux[0:64, 512:512 + T], Act.Copy), [t_xpm])
            t_bce = wop("scalar", lambda e: e.activation(
                bcs[:], paux[64:96, 512:512 + T], Act.Copy), [t_xpm])
            t_bcw = wdma("sync", lambda e, co=co: e.dma_start(
                out=bcrows[:, co:co + T], in_=bcs[:]), [t_bce])
            t_xp_last = t_xpm

            t_bmc = [None] * 32
            for n in range(32):
                t_bmc[n] = wdma("sync", lambda e, n=n, co=co: e.dma_start(
                    out=bmc[:, n * T:(n + 1) * T],
                    in_=bcrows[n:n + 1, co:co + T].partition_broadcast(128)),
                    [t_bcw if n == 0 else None,
                     t_bmc_cons_v if n == 0 else None,
                     t_bmc_cons_g if n == 0 else None])

            t_y = [None] * 16
            t_vsc = [None] * 8
            t_hsc = [None] * 8
            t_hcp = [None] * 8
            t_delta = [None] * 16
            t_u = [None] * 16
            t_sp4 = None
            t_hc15 = None
            t_mi15 = None
            t_umi_guard = [t_umi_guard_c[0], t_umi_guard_c[1]]
            t_y2_slot = list(t_y2_slot_c)
            t_ymm15_prev = t_ymm15_c

            def stage_dt(j):
                nonlocal t_sp4
                dslot = (j % 2) * T
                t_dt = wop("tensor", lambda e, j=j: e.matmul(
                    paux[:, 0:T], wa_dt[:, j * 128:(j + 1) * 128], dsb[:],
                    start=True, stop=True),
                    [t_dsb if j == 0 else None,
                     w_tok if (c == 0 and j == 0) else None,
                     t_inv if j == 0 else None,
                     t_sp4 if t_sp4 is not None else None])
                dtb_col = cvec[:, DTB0 + l * 16 + j: DTB0 + l * 16 + j + 1]
                sp1 = wop("scalar", lambda e, dtb_col=dtb_col: e.activation(
                    spa[:], paux[:, 0:T], Act.Abs, bias=dtb_col),
                    [t_dt, t_delta[j - 1] if j > 0 else t_delta_prev])
                sp2 = wop("scalar", lambda e: e.activation(
                    spa[:], spa[:], Act.Exp, scale=-1.0), [sp1])
                sp3 = wop("scalar", lambda e: e.activation(
                    spa[:], spa[:], Act.Ln, bias=1.0), [sp2])
                t_sp4 = wop("scalar", lambda e, dtb_col=dtb_col: e.activation(
                    spb[:], paux[:, 0:T], Act.Relu, bias=dtb_col), [sp3])
                t_delta[j] = wop("vector", lambda e, dslot=dslot: e.tensor_tensor(
                    dfl[:, dslot:dslot + T], spa[:], spb[:], Alu.add),
                    [t_sp4, prev_da15[j % 2], prev_u[j % 2], DBGB()])
                if dbg and l == 0 and c == 0:
                    t_dbgl.append(wdma("sync", lambda e, j=j, dslot=dslot: e.dma_start(
                        out=dbgs["delta"][j * 128:(j + 1) * 128, 0:T],
                        in_=dfl[:, dslot:dslot + T]), [t_delta[j]]))
                t_u[j] = wop("vector", lambda e, j=j, dslot=dslot: e.tensor_tensor(
                    ubf[:, dslot:dslot + T], dfl[:, dslot:dslot + T],
                    xbt[:, j * T:(j + 1) * T], Alu.mult),
                    [t_delta[j], t_umi_guard[j % 2]])
                prev_u[j % 2] = t_u[j]

            stage_dt(0)
            for j in range(16):
                dslot = (j % 2) * T
                yslot = (j % 2) * 512
                if j + 1 < 16:
                    stage_dt(j + 1)
                t_hc = [None] * NST
                t_ymm = [None] * NST
                t_mi_last = None
                t_v_last = None
                t_h_last = None
                for g in range(8):
                    n0 = 2 * g
                    n1 = 2 * g + 1
                    go = (g % 2) * GW
                    t_da0 = wop("scalar", lambda e, n0=n0, go=go, dslot=dslot, j=j:
                                e.activation(
                                    dag[:, go:go + T], dfl[:, dslot:dslot + T],
                                    Act.Exp,
                                    scale=nega[:, l * 256 + j * 16 + n0:
                                               l * 256 + j * 16 + n0 + 1]),
                                [t_delta[j],
                                 t_na if (c == 0 and j == 0 and g == 0) else None,
                                 t_zdag if (c == 0 and j == 0 and g < 2) else None,
                                 t_hsc[g - 2] if g >= 2 else
                                 (prev_hsc_last if g == 0 else None)])
                    t_da1 = wop("scalar", lambda e, n1=n1, go=go, dslot=dslot, j=j:
                                e.activation(
                                    dag[:, go + T + 1:go + 2 * T + 1],
                                    dfl[:, dslot:dslot + T], Act.Exp,
                                    scale=nega[:, l * 256 + j * 16 + n1:
                                               l * 256 + j * 16 + n1 + 1]),
                                [t_da0])
                    t_mi0 = wop("gpsimd", lambda e, n0=n0, go=go, dslot=dslot:
                                e.tensor_tensor(
                                    mig[:, go:go + T], ubf[:, dslot:dslot + T],
                                    bmc[:, n0 * T:(n0 + 1) * T], Alu.mult),
                                [t_u[j], t_bmc[n0],
                                 t_zmig if (c == 0 and j == 0 and g < 2) else None,
                                 t_vsc[g - 2] if g >= 2 else
                                 (prev_vsc_last if g == 0 else None)])
                    t_mi1 = wop("gpsimd", lambda e, n1=n1, go=go, dslot=dslot:
                                e.tensor_tensor(
                                    mig[:, go + T + 1:go + 2 * T + 1],
                                    ubf[:, dslot:dslot + T],
                                    bmc[:, n1 * T:(n1 + 1) * T], Alu.mult),
                                [t_mi0, t_bmc[n1]])
                    t_sv = wop("gpsimd", lambda e, n1=n1, go=go, j=j: e.tensor_copy(
                        mig[:, go + T:go + T + 1],
                        vcar[:, j * 16 + n1:j * 16 + n1 + 1]),
                        [t_mi1, prev_vext[j] if c > 0 else
                         (t_zc1 if j == 0 and g == 0 else None)])
                    t_vsc[g] = wop("vector", lambda e, go=go, n0=n0, j=j:
                                   e.tensor_tensor_scan(
                                       vg[:, go:go + GW], beta2[:],
                                       mig[:, go:go + GW],
                                       vcar[:, j * 16 + n0:j * 16 + n0 + 1],
                                       Alu.mult, Alu.add),
                                   [t_sv,
                                    t_cb if (c == 0 and j == 0 and g == 0) else None,
                                    t_hsc[g - 2] if g >= 2 else
                                    (prev_hsc_last if g == 0 else None)])
                    t_sh = wop("vector", lambda e, n1=n1, go=go, j=j: e.tensor_copy(
                        vg[:, go + T:go + T + 1],
                        hcar[:, j * 16 + n1:j * 16 + n1 + 1]),
                        [t_vsc[g], t_hext_g if c > 0 or j > 0 or g > 0 else t_zc2])
                    t_hsc[g] = wop("vector", lambda e, go=go, n0=n0, j=j:
                                   e.tensor_tensor_scan(
                                       hg[:, go:go + GW], dag[:, go:go + GW],
                                       vg[:, go:go + GW],
                                       hcar[:, j * 16 + n0:j * 16 + n0 + 1],
                                       Alu.mult, Alu.add),
                                   [t_sh, t_da1,
                                    t_hcp[g - 2] if g >= 2 else
                                    (prev_hcp_last if g == 0 else None)])
                    t_vext_g = wop("vector", lambda e, go=go, j=j, n0=n0: e.tensor_copy(
                        vcar[:, j * 16 + n0:j * 16 + n0 + 2],
                        vg[:, go + T - 1::T + 1][:, 0:2]), [t_hsc[g]])
                    t_hext_g = wop("vector", lambda e, go=go, j=j, n0=n0: e.tensor_copy(
                        hcar[:, j * 16 + n0:j * 16 + n0 + 2],
                        hg[:, go + T - 1::T + 1][:, 0:2]), [t_hsc[g]])
                    for s in range(2):
                        n = n0 + s
                        hc_o = (n % 2) * T
                        t_hc[n] = wop("vector", lambda e, n=n, go=go, s=s, hc_o=hc_o:
                                      e.tensor_tensor(
                                          hcl[:, hc_o:hc_o + T],
                                          hg[:, go + s * (T + 1):go + s * (T + 1) + T],
                                          bmc[:, (16 + n) * T:(17 + n) * T], Alu.mult),
                                      [t_hsc[g], t_bmc[16 + n],
                                       t_ymm[n - 2] if n >= 2 else
                                       (t_ymm15_prev if n == 0 else None)])
                        t_ymm[n] = wop("tensor", lambda e, n=n, hc_o=hc_o, yslot=yslot:
                                       e.matmul(
                                           pout[:, yslot:yslot + T], identb[:],
                                           hcl[:, hc_o:hc_o + T],
                                           start=(n == 0), stop=(n == 15)),
                                       [t_hc[n],
                                        t_y2_slot[j % 2] if n == 0 else None])
                    t_hcp[g] = t_hc[n1]
                    t_mi_last = t_mi1
                prev_vsc_last = t_vsc[7]
                prev_hsc_last = t_hsc[7]
                prev_hcp_last = t_hc[15]
                prev_da15[j % 2] = t_da1
                prev_vext[j] = t_vext_g
                t_umi_guard[j % 2] = t_mi_last
                t_hc15 = t_hc[15]
                t_mi15 = t_mi_last
                t_ymm15_prev = t_ymm[15]

                t_y2 = wop("vector", lambda e, j=j, yslot=yslot: e.scalar_tensor_tensor(
                    yfl[:], xbt[:, j * T:(j + 1) * T],
                    cvec[:, DP0 + l * 16 + j: DP0 + l * 16 + j + 1],
                    pout[:, yslot:yslot + T], Alu.mult, Alu.add), [t_ymm[15]])
                t_y2_slot[j % 2] = t_y2
                t_y[j] = wop("vector", lambda e, j=j: e.tensor_tensor(
                    ybl[:, j * T:(j + 1) * T], yfl[:], szt[:, j * T:(j + 1) * T],
                    Alu.mult),
                    [t_y2, t_evac[16 + j], t_outmm_prev if j == 0 else None, DBGB()])
                if dbg and l == 0 and c == 0:
                    t_dbgl.append(wdma("sync", lambda e, j=j: e.dma_start(
                        out=dbgs["y"][j * 128:(j + 1) * 128, 0:T],
                        in_=ybl[:, j * T:(j + 1) * T]), [t_y[j]]))
            t_umi_guard_c = t_umi_guard
            t_y2_slot_c = t_y2_slot
            t_ymm15_c = t_ymm15_prev
            t_scan_tail = t_y[15]
            t_bmc_cons_v = t_hc15
            t_bmc_cons_g = t_mi15
            t_sp_last_prev = t_sp4

            t_ev = [None] * 8
            t_om = None
            for dt_i in range(8):
                po = (dt_i % 2) * 512
                ro = (dt_i % 2) * T
                t_xr = wdma("sync", lambda e, dt_i=dt_i, co=co, ro=ro: e.dma_start(
                    out=xre[:, ro:ro + T],
                    in_=src_ap[dt_i * 128:(dt_i + 1) * 128, co:co + T]),
                    [t_ev[dt_i - 2] if dt_i >= 2 else t_xre_guard])
                for j in range(16):
                    t_om = wop("tensor", lambda e, j=j, dt_i=dt_i, po=po: e.matmul(
                        pout[:, po:po + T],
                        wa_out[:, (j * 8 + dt_i) * 128:(j * 8 + dt_i + 1) * 128],
                        ybl[:, j * T:(j + 1) * T], start=(j == 0), stop=(j == 15)),
                        [t_y[15] if (dt_i == 0 and j == 0) else None,
                         t_y2_slot_c[dt_i % 2] if j == 0 else None,
                         (t_ev[dt_i - 2] if dt_i >= 2 else t_outevac_prev)
                         if j == 0 else None])
                t_ev[dt_i] = wop("vector", lambda e, dt_i=dt_i, po=po, ro=ro:
                                 e.scalar_tensor_tensor(
                                     xre[:, ro:ro + T], pout[:, po:po + T],
                                     1.0, xre[:, ro:ro + T],
                                     Alu.mult, Alu.add), [t_om, t_xr])
                t_x_prev_store[dt_i] = wdma("sync", lambda e, dt_i=dt_i, co=co, ro=ro:
                                            e.dma_start(
                    out=dst_ap[dt_i * 128:(dt_i + 1) * 128, co:co + T],
                    in_=xre[:, ro:ro + T]), [t_ev[dt_i]])
            t_xre_guard = t_ev[7]
            t_outmm_prev = t_om
            t_outevac_prev = t_ev[7]
            dst_store[c] = t_x_prev_store[7]
        return dst_store, t_sp_last_prev, t_inproj_last, t_om, t_xp_last

    storeA, t_spA, t_li, t_lo, t_lx = mamba_layer(0, xfm, xresA, t_wl0, None)
    t_wl1 = load_mamba_weights(1, war_in=t_li, war_out=t_lo, war_xpdt=t_lx)
    storeB, t_spB, t_li, t_lo, t_lx = mamba_layer(1, xresA, xresB, t_wl1, storeA)

    t_war = ("w", "tensor", W.cnt["tensor"])
    first = [t_li]
    for k in range(8):
        for cc in range(16):
            wdma("gpsimd", lambda e, k=k, cc=cc: e.dma_start(
                out=wa_in[:, (k * 16 + cc) * 128:(k * 16 + cc + 1) * 128],
                in_=w_w1[k, cc]), first)
            first = []
    for k in range(16):
        for cc in range(8):
            wdma("gpsimd", lambda e, k=k, cc=cc: e.dma_start(
                out=wa_in[:, 16384 + (k * 8 + cc) * 128:16384 + (k * 8 + cc + 1) * 128],
                in_=w_w2[k, cc]))
    first = [t_lo]
    for k in range(8):
        wdma("gpsimd", lambda e, k=k: e.dma_start(out=wa_g[:, k * 8:(k + 1) * 8],
                                                  in_=w_wg[k]), first)
        first = []
    for cc in range(8):
        wdma("gpsimd", lambda e, cc=cc: e.dma_start(
            out=wa_b2[:, cc * 128:(cc + 1) * 128], in_=w_b2[cc]))
    t_wf = ("d", "gpsimd", W.dmac["gpsimd"])

    rowguard = {"r1": None, "r2": None}

    def postln(c, t_src, guards, invb_guard):
        co = c * T
        t_s1 = None
        t_s2 = None
        t_s2l = [None] * 8
        for jd in range(8):
            t_s1 = wop("tensor", lambda e, jd=jd: e.matmul(
                paux[0:1, 0:T], onecol[:], xf[:, jd * T:(jd + 1) * T],
                start=(jd == 0), stop=(jd == 7)),
                [t_src[jd]] + (guards if jd == 0 else []))
            buf = tmpa if jd % 2 == 0 else tmpb
            tq = wop("scalar", lambda e, jd=jd, buf=buf: e.activation(
                buf[:], xf[:, jd * T:(jd + 1) * T], Act.Square),
                [t_src[jd], t_s2l[jd - 2] if jd >= 2 else None])
            t_s2 = wop("tensor", lambda e, jd=jd, buf=buf: e.matmul(
                paux[0:1, 512:512 + T], onecol[:], buf[:],
                start=(jd == 0), stop=(jd == 7)),
                [tq] + (guards if jd == 0 else []))
            t_s2l[jd] = t_s2
        t_m = wop("vector", lambda e: e.tensor_scalar_mul(
            rowsb[0:1, 0:T], paux[0:1, 0:T], 1.0 / D),
            [t_s1, rowguard["r1"], rowguard["r2"]])
        t_v2 = wop("vector", lambda e: e.tensor_tensor(
            rowsb[0:1, T:2 * T], rowsb[0:1, 0:T], rowsb[0:1, 0:T], Alu.mult), [t_m])
        t_v3 = wop("vector", lambda e: e.scalar_tensor_tensor(
            rowsb[0:1, T:2 * T], paux[0:1, 512:512 + T], 1.0 / D,
            rowsb[0:1, T:2 * T], Alu.mult, Alu.subtract), [t_v2, t_s2])
        t_iv_a = wop("scalar", lambda e: e.activation(
            rowsb[0:1, 2 * T:3 * T], rowsb[0:1, T:2 * T], Act.Sqrt, bias=EPS_LN), [t_v3])
        t_iv = wop("vector", lambda e: e.reciprocal(
            rowsb[0:1, T:2 * T], rowsb[0:1, 2 * T:3 * T]), [t_iv_a])
        t_mi2 = wop("vector", lambda e: e.tensor_tensor(
            rowsb[0:1, 0:T], rowsb[0:1, 0:T], rowsb[0:1, T:2 * T], Alu.mult), [t_iv])
        t_r1 = wdma("sync", lambda e, co=co: e.dma_start(
            out=rowscr[1:2, co:co + T], in_=rowsb[0:1, T:2 * T]), [t_iv])
        t_r2 = wdma("sync", lambda e, co=co: e.dma_start(
            out=rowscr[2:3, co:co + T], in_=rowsb[0:1, 0:T]), [t_mi2])
        rowguard["r1"] = t_r1
        rowguard["r2"] = t_r2
        t_b1_ = wdma("scalar", lambda e, co=co: e.dma_start(
            out=invb[:, 0:T], in_=rowscr[1:2, co:co + T].partition_broadcast(128)),
            [t_r1, invb_guard])
        t_b2_ = wdma("scalar", lambda e, co=co: e.dma_start(
            out=invb[:, T:2 * T], in_=rowscr[2:3, co:co + T].partition_broadcast(128)),
            [t_r2])
        t_out = []
        for jd in range(8):
            t1 = wop("vector", lambda e, jd=jd: e.tensor_tensor(
                xf[:, jd * T:(jd + 1) * T], xf[:, jd * T:(jd + 1) * T],
                invb[:, 0:T], Alu.mult), [t_b1_, t_mi2 if jd == 0 else None])
            t2 = wop("vector", lambda e, jd=jd: e.tensor_tensor(
                xf[:, jd * T:(jd + 1) * T], xf[:, jd * T:(jd + 1) * T],
                invb[:, T:2 * T], Alu.subtract), [t1, t_b2_ if jd == 0 else None])
            t_out.append(t2)
        return t_out, t_v3, t_out[7]

    entry2 = W.barrier_tokens()
    t_xmid_store = [None] * NCH
    t_gtok = [None] * TT
    t_xf_store = [None] * 8
    t_x0_cons = None
    t_xl_cons = None
    t_xlc_prev = None
    t_gate_xf_cons = None
    t_invb_guard = ("w", "vector", W.cnt["vector"])
    sq_guards = list(entry2)
    for c in range(NCH):
        co = c * T
        t_x0 = []
        t_xs = []
        for jd in range(8):
            t_x0.append(wdma("sync", lambda e, jd=jd, co=co: e.dma_start(
                out=x0f[:, jd * T:(jd + 1) * T],
                in_=xfm[jd * 128:(jd + 1) * 128, co:co + T]),
                [t_x0_cons if jd == 0 else None]
                + (entry2 if (c == 0 and jd == 0) else [])))
            t_xs.append(wdma("sync", lambda e, jd=jd, co=co: e.dma_start(
                out=xf[:, jd * T:(jd + 1) * T],
                in_=xresB[jd * 128:(jd + 1) * 128, co:co + T]),
                [storeB[c] if jd == 0 else None, t_xf_store[jd],
                 t_xlc_prev if jd == 0 else None,
                 t_gate_xf_cons if jd == 0 else None]))
        t_s = []
        for jd in range(8):
            t_s.append(wop("vector", lambda e, jd=jd: e.tensor_tensor(
                xf[:, jd * T:(jd + 1) * T], xf[:, jd * T:(jd + 1) * T],
                x0f[:, jd * T:(jd + 1) * T], Alu.add), [t_x0[jd], t_xs[jd]]))
        t_x0_cons = t_s[7]
        t_xm, t_stats_free, t_ln_last = postln(c, t_s, sq_guards, t_invb_guard)
        t_invb_guard = t_ln_last
        sq_guards = []
        for jd in range(8):
            t_xf_store[jd] = wdma("sync", lambda e, jd=jd, co=co: e.dma_start(
                out=xmid[jd * 128:(jd + 1) * 128, co:co + T],
                in_=xf[:, jd * T:(jd + 1) * T]), [t_xm[jd]])
            wop("scalar", lambda e, jd=jd: e.activation(
                xl[:, jd * T:(jd + 1) * T], xf[:, jd * T:(jd + 1) * T], Act.Copy),
                [t_xm[jd], t_xl_cons if jd == 0 else None])
        t_xlc = ("w", "scalar", W.cnt["scalar"])
        t_xlc_prev = t_xlc
        t_xmid_store[c] = t_xf_store[7]

        t_g = None
        for k in range(8):
            t_g = wop("tensor", lambda e, k=k: e.matmul(
                paux[0:8, 512:512 + T], wa_g[:, k * 8:(k + 1) * 8],
                xf[:, k * T:(k + 1) * T], start=(k == 0), stop=(k == 7)),
                [t_xm[k],
                 t_wf if (c == 0 and k == 0) else None,
                 t_stats_free if k == 0 else None])
        t_xl_cons = t_g
        t_gate_xf_cons = t_g
        t_ge = wop("vector", lambda e: e.tensor_scalar_add(
            glg[:, 0:T], paux[0:8, 512:512 + T], wgbc[:]),
            [t_g, t_wgb if c == 0 else None])
        sq_guards = [t_ge]
        for r in range(NTI):
            rg = c * NTI + r
            t_tr = wop("tensor", lambda e, r=r: e.transpose(
                pmm[0:128, 0:8], glg[0:8, r * 128:(r + 1) * 128], ident[0:8, 0:8]),
                [t_ge, t_id if rg == 0 else None,
                 t_gtok[rg - 1] if rg >= 1 else None])
            t_gtok[rg] = wop("scalar", lambda e, rg=rg: e.activation(
                gt[:, rg * 8:(rg + 1) * 8], pmm[0:128, 0:8], Act.Copy), [t_tr])
    if dbg:
        wdma("sync", lambda e: e.dma_start(out=dbgs["gt"][:], in_=gt[:]),
             [t_gtok[TT - 1]])

    t_z = wop("vector", lambda e: e.memset(m2acc[:], 0.0))
    t_m2t = [None] * TT
    t_m2w = None
    t_ac = None
    for r in range(TT):
        t_mx = wop("vector", lambda e, r=r: e.tensor_reduce(
            gtt[:, 8:9], gt[:, r * 8:(r + 1) * 8], AX.X, Alu.max),
            [t_gtok[r], t_m2w, t_ac])
        t_m1 = wop("vector", lambda e, r=r: e.tensor_scalar(
            gtt[:, 0:8], gt[:, r * 8:(r + 1) * 8], gtt[:, 8:9], None, Alu.is_ge), [t_mx])
        t_gm = wop("vector", lambda e, r=r: e.scalar_tensor_tensor(
            gtt[:, 0:8], gtt[:, 0:8], -1e30, gt[:, r * 8:(r + 1) * 8],
            Alu.mult, Alu.add), [t_m1])
        t_mx2 = wop("vector", lambda e: e.tensor_reduce(
            gtt[:, 9:10], gtt[:, 0:8], AX.X, Alu.max), [t_gm])
        t_m2 = wop("vector", lambda e, r=r: e.tensor_scalar(
            gtt[:, 0:8], gt[:, r * 8:(r + 1) * 8], gtt[:, 9:10], None, Alu.is_ge),
            [t_mx2])
        t_ac = wop("vector", lambda e: e.tensor_tensor(
            m2acc[:], m2acc[:], gtt[:, 0:8], Alu.add), [t_m2, t_z if r == 0 else None])
        t_tr = wop("tensor", lambda e, r=r: e.transpose(
            pmm[0:8, 512:640], gtt[:, 0:8], ident[:]),
            [t_m2, t_m2t[r - 1] if r >= 1 else None])
        t_m2t[r] = wop("scalar", lambda e, r=r: e.activation(
            self32[:, r * 128:(r + 1) * 128], pmm[0:8, 512:640], Act.Copy), [t_tr])
        t_m2w = t_m2t[r]
    t_trn = wop("tensor", lambda e: e.transpose(pmm[0:8, 0:128], m2acc[:], ident[:]),
                [t_ac, t_m2t[TT - 1]])
    t_ne = wop("scalar", lambda e: e.activation(nptt[:], pmm[0:8, 0:128], Act.Copy),
               [t_trn])
    t_np = wop("vector", lambda e: e.tensor_reduce(
        npsb[:, 0:1], nptt[:], AX.X, Alu.add), [t_ne])
    t_def = wop("vector", lambda e: e.tensor_scalar(
        npsb[:, 1:2], npsb[:, 0:1], -1.0, float(K_SEL), Alu.mult, Alu.add), [t_np])
    if dbg:
        wdma("sync", lambda e: e.dma_start(out=dbgs["np"][:, 0:2], in_=npsb[:]), [t_def])
    t_selbf = [None] * NCH
    t_ec = None
    for c in range(NCH):
        co = c * T
        t_ez = wop("vector", lambda e, co=co: e.tensor_scalar(
            ezb[:, 0:T], self32[:, co:co + T], -1.0, 1.0, Alu.mult, Alu.add),
            [t_def if c == 0 else None])
        t_zc = wop("vector", lambda e, c=c: e.tensor_tensor_scan(
            ezb[:, T:2 * T], onest[0:8, 0:T], ezb[:, 0:T],
            (0.0 if c == 0 else ecar[:]), Alu.mult, Alu.add), [t_ez, t_ec])
        t_ec = wop("vector", lambda e: e.tensor_copy(ecar[:], ezb[:, 2 * T - 1:2 * T]),
                   [t_zc])
        t_zs = wop("vector", lambda e: e.tensor_scalar(
            ezb[:, T:2 * T], ezb[:, T:2 * T], npsb[:, 1:2], None, Alu.is_le), [t_ec])
        t_zs2 = wop("vector", lambda e: e.tensor_tensor(
            ezb[:, T:2 * T], ezb[:, T:2 * T], ezb[:, 0:T], Alu.mult), [t_zs])
        t_sel = wop("vector", lambda e, co=co: e.tensor_tensor(
            self32[:, co:co + T], self32[:, co:co + T], ezb[:, T:2 * T], Alu.add),
            [t_zs2])
        t_selbf[c] = wdma("sync", lambda e, co=co: e.dma_start(
            out=selrow_d[:, co:co + T], in_=self32[:, co:co + T]), [t_sel])

    entry3 = W.barrier_tokens()
    t_hev_prev = None
    t_w2_last_prev = None
    t_oev_guards = list(entry3)
    t_sbc_cons = None
    t_ost = [None] * 8
    t_invb_guard2 = t_invb_guard
    t_x0_g = t_x0_cons
    last_store = None
    for c in range(NCH):
        co = c * T
        t_xm_l = []
        t_x0_l = []
        for jd in range(8):
            t_xm_l.append(wdma("gpsimd", lambda e, jd=jd, co=co: e.dma_start(
                out=xl[:, jd * T:(jd + 1) * T],
                in_=xmid[jd * 128:(jd + 1) * 128, co:co + T]),
                [t_xmid_store[c] if jd == 0 else None,
                 t_xl_cons if jd == 0 else None]))
            t_x0_l.append(wdma("sync", lambda e, jd=jd, co=co: e.dma_start(
                out=x0f[:, jd * T:(jd + 1) * T],
                in_=xmid[jd * 128:(jd + 1) * 128, co:co + T]),
                [t_x0_g if jd == 0 else None]))
        t_sbc = [None] * 8
        for ee in range(4):
            t_sbc[ee] = wdma("scalar", lambda e, ee=ee, co=co: e.dma_start(
                out=sbc[:, (ee % 4) * T:(ee % 4 + 1) * T],
                in_=selrow_d[ee:ee + 1, co:co + T].partition_broadcast(128)),
                [t_selbf[c] if ee == 0 else None,
                 t_sbc_cons if ee == 0 else None])
        t_hm = [None] * 16
        for ht in range(16):
            po = (ht % 2) * 512
            t_mmh = None
            for k in range(8):
                t_mmh = wop("tensor", lambda e, ht=ht, k=k, po=po: e.matmul(
                    pmm[:, po:po + T],
                    wa_in[:, (k * 16 + ht) * 128:(k * 16 + ht + 1) * 128],
                    xl[:, k * T:(k + 1) * T], start=(k == 0), stop=(k == 7)),
                    [t_xm_l[k] if ht == 0 else None,
                     t_wf if (c == 0 and ht == 0 and k == 0) else None,
                     (t_hm[ht - 2] if ht >= 2 else t_hev_prev)
                     if k == 0 else None]
                    + (entry3 if (c == 0 and ht == 0 and k == 0) else []))
            t_he = wop("scalar", lambda e, ht=ht, po=po: e.activation(
                xbt[:, ht * T:(ht + 1) * T], pmm[:, po:po + T], Act.Gelu,
                bias=b1c[:, ht:ht + 1]),
                [t_mmh, t_vecs if (c == 0 and ht == 0) else None,
                 t_w2_last_prev if ht == 0 else None])
            t_hm[ht] = wop("vector", lambda e, ht=ht: e.tensor_tensor(
                xbt[:, ht * T:(ht + 1) * T], xbt[:, ht * T:(ht + 1) * T],
                sbc[:, ((ht // 2) % 4) * T:((ht // 2) % 4 + 1) * T], Alu.mult),
                [t_he, t_sbc[ht // 2]])
            if ht % 2 == 1 and ht // 2 + 4 < 8:
                ee2 = ht // 2 + 4
                t_sbc[ee2] = wdma("scalar", lambda e, ee2=ee2, co=co: e.dma_start(
                    out=sbc[:, (ee2 % 4) * T:(ee2 % 4 + 1) * T],
                    in_=selrow_d[ee2:ee2 + 1, co:co + T].partition_broadcast(128)),
                    [t_hm[ht]])
        t_hev_prev = t_hm[15]
        t_sbc_cons = t_hm[15]
        t_xl_cons = t_mmh
        t_w2 = None
        t_s2 = [None] * 8
        for dt_i in range(8):
            po = (dt_i % 2) * 512
            for k in range(16):
                t_w2 = wop("tensor", lambda e, dt_i=dt_i, k=k, po=po: e.matmul(
                    pout[:, po:po + T],
                    wa_in[:, 16384 + (k * 8 + dt_i) * 128:
                          16384 + (k * 8 + dt_i + 1) * 128],
                    xbt[:, k * T:(k + 1) * T], start=(k == 0), stop=False),
                    [t_hm[k] if dt_i == 0 else None,
                     (t_s2[dt_i - 2] if dt_i >= 2 else None) if k == 0 else None]
                    + (t_oev_guards if (dt_i == 0 and k == 0) else []))
            t_w2 = wop("tensor", lambda e, dt_i=dt_i, co=co, po=po: e.matmul(
                pout[:, po:po + T],
                wa_b2[:, dt_i * 128:(dt_i + 1) * 128],
                self32[:, co:co + T], start=False, stop=True),
                [t_selbf[c] if dt_i == 0 else None])
            t_s2[dt_i] = wop("vector", lambda e, dt_i=dt_i, po=po: e.scalar_tensor_tensor(
                xf[:, dt_i * T:(dt_i + 1) * T], pout[:, po:po + T],
                1.0, x0f[:, dt_i * T:(dt_i + 1) * T], Alu.mult, Alu.add),
                [t_w2, t_x0_l[dt_i], t_ost[dt_i] if c > 0 else None])
        t_w2_last_prev = t_w2
        t_x0_g = t_s2[7]
        t_fo, t_sf, t_ln_last = postln(c, t_s2, sq_guards, t_invb_guard2)
        sq_guards = []
        t_invb_guard2 = t_ln_last
        t_oev_guards = [t_ln_last]
        for jd in range(8):
            t_ost[jd] = wdma("sync", lambda e, jd=jd, co=co: e.dma_start(
                out=ofm[jd * 128:(jd + 1) * 128, co:co + T],
                in_=xf[:, jd * T:(jd + 1) * T]), [t_fo[jd]])
            last_store = t_ost[jd]

    for e in ENGS:
        if W.dmac[e] > 0:
            W.recs[e].append((None, [("d", e, W.dmac[e])], None))

    with es, nc.Block() as blk:
        def mkbody(eng_name):
            def body(eh):
                have = {}
                for key in [("w", x) for x in ENGS] + [("d", x) for x in ENGS]:
                    have[key] = 0
                for build, waits, isdma in W.recs[eng_name]:
                    for kind, peng, val in waits[:-1]:
                        if have[(kind, peng)] >= val:
                            continue
                        have[(kind, peng)] = val
                        sem = S[peng] if kind == "w" else Q[peng]
                        eh.wait_ge(sem, val)
                    attach = None
                    if waits:
                        kind, peng, val = waits[-1]
                        if have[(kind, peng)] < val:
                            have[(kind, peng)] = val
                            attach = (S[peng] if kind == "w" else Q[peng], val)
                    if build is None:
                        if attach is not None:
                            eh.wait_ge(attach[0], attach[1])
                        continue
                    inst = build(eh)
                    if attach is not None:
                        inst._wait_ge(attach[0], attach[1])
                    if isdma:
                        inst.then_inc(Q[eng_name], 16)
                    else:
                        inst.then_inc(S[eng_name], 1)
            return body

        blk.sync(mkbody("sync"))
        blk.tensor(mkbody("tensor"))
        blk.scalar(mkbody("scalar"))
        blk.vector(mkbody("vector"))
        blk.gpsimd(mkbody("gpsimd"))

    return nc


def _pack(inputs, L, T):
    f32 = np.float32
    ipw = np.asarray(inputs["L0_attn_in_proj_w"], f32)
    ow = np.asarray(inputs["L0_attn_out_w"], f32)
    xpw = np.asarray(inputs["L0_attn_x_proj_w"], f32)
    dtw = np.asarray(inputs["L0_attn_dt_w"], f32)
    W1 = np.asarray(inputs["L0_ffn_W1"], f32)
    W2 = np.asarray(inputs["L0_ffn_W2"], f32)
    wg = np.asarray(inputs["L0_ffn_w_gate_w"], f32)
    b2 = np.asarray(inputs["L0_ffn_b2"], f32)

    d = {}
    t = ipw.reshape(2, 32, 128, 8, 128)
    d["w_in"] = np.ascontiguousarray(t.transpose(0, 3, 1, 4, 2))
    t = ow.reshape(2, 8, 128, 16, 128)
    d["w_out"] = np.ascontiguousarray(t.transpose(0, 3, 1, 4, 2))
    t = xpw.reshape(2, 96, 16, 128)
    d["w_xp"] = np.ascontiguousarray(t.transpose(0, 2, 3, 1))
    t = dtw.reshape(2, 16, 128, 64)
    d["w_dt"] = np.ascontiguousarray(t.transpose(0, 1, 3, 2))
    W1c = np.ascontiguousarray(W1.transpose(1, 0, 2).reshape(1024, 2048))
    t = W1c.reshape(8, 128, 16, 128)
    d["w_w1"] = np.ascontiguousarray(t.transpose(0, 2, 1, 3))
    W2c = np.ascontiguousarray(W2.reshape(2048, 1024))
    t = W2c.reshape(16, 128, 8, 128)
    d["w_w2"] = np.ascontiguousarray(t.transpose(0, 2, 1, 3))
    d["w_wg"] = np.ascontiguousarray(wg.reshape(8, 128, 8))
    d["w_b2"] = np.ascontiguousarray(b2.reshape(8, 8, 128).transpose(1, 0, 2))

    d["cw_v"] = np.ascontiguousarray(
        np.asarray(inputs["L0_attn_conv_w"], f32).reshape(2, 16, 128, 4))
    d["cb_v"] = np.ascontiguousarray(
        np.asarray(inputs["L0_attn_conv_b"], f32).reshape(2, 16, 128, 1))
    d["dtb_v"] = np.ascontiguousarray(
        np.asarray(inputs["L0_attn_dt_b"], f32).reshape(2, 16, 128, 1))
    d["dp_v"] = np.ascontiguousarray(
        np.asarray(inputs["L0_attn_Dp"], f32).reshape(2, 16, 128, 1))
    d["nw_v"] = np.ascontiguousarray(
        np.asarray(inputs["L0_attn_norm_w"], f32).reshape(2, 8, 128, 1))
    d["alog_v"] = np.ascontiguousarray(
        np.asarray(inputs["L0_attn_A_log"], f32).reshape(2, 16, 128, 16))
    d["b1_v"] = np.ascontiguousarray(
        np.asarray(inputs["L0_ffn_b1"], f32).reshape(16, 128, 1))
    d["wgb_v"] = np.ascontiguousarray(
        np.asarray(inputs["L0_ffn_w_gate_b"], f32).reshape(8, 1))
    cb2 = np.full((128, 2 * (T + 1)), BETA, f32)
    cb2[:, T] = 0.0
    cb2[:, 2 * T + 1] = 0.0
    d["cbeta"] = cb2
    d["cmisc"] = np.ascontiguousarray(
        np.broadcast_to(np.array([1e-5, 1.0 / 1024.0], f32), (128, 2)))
    d["cones"] = np.ones((128, T), ml_dtypes.bfloat16)
    d["cident"] = np.eye(128, dtype=f32)
    d["cidentb"] = np.eye(128, dtype=ml_dtypes.bfloat16)
    return d


def kernel(**inputs):
    L = 4096
    T = 256
    x = np.asarray(inputs["L0_attn_x"], np.float32)
    B = x.shape[0]
    shared = _pack(inputs, L, T)
    nc = build_program(L, T, dbg=False)
    in_maps = []
    for b in range(B):
        m = dict(shared)
        m["xfm"] = np.ascontiguousarray(x[b].T)
        in_maps.append(m)
    res = run_bass_kernel_spmd(nc, in_maps, core_ids=list(range(B)))
    out = np.stack([np.asarray(r["ofm"]).T for r in res.results], axis=0)
    return np.ascontiguousarray(out.astype(np.float32))
